# version 04
# baseline (speedup 1.0000x reference)
import numpy as np
from contextlib import ExitStack

import concourse.bass as bass
import concourse.mybir as mybir
from concourse.bass_utils import run_bass_kernel_spmd

F32 = mybir.dt.float32
BF16 = mybir.dt.bfloat16
FP16 = mybir.dt.float16
AF = mybir.ActivationFunctionType
ALU = mybir.AluOpType
AX = mybir.AxisListType

B, T, D = 4, 4096, 2048
TL = T // 2
HN, HL, DH = 16, 8, 128
NKD = D // 128
SW = 512
NSG = T // SW
NSL = TL // SW
GUH = 16384
HID = 8192
NH_CH = HID // 128
LN_EPS = 1e-5
RMS_EPS = float(np.finfo(np.float32).eps)
LAM_INIT = float(np.float32(0.8) - np.float32(0.6) * np.float32(np.exp(np.float32(-3.6))))
ONE_MINUS_LAM = float(np.float32(1.0) - np.float32(LAM_INIT))
QTHR = 0.5 + 2.0 ** -11


class Sem:
    def __init__(self, nc, es, name):
        self.h = es.enter_context(nc.semaphore(name))
        self.n = 0

    def inc(self, inst, k=1):
        inst.then_inc(self.h, k)
        self.n += k
        return self.n

    def incc(self, inst):
        inst.then_inc(self.h)
        self.n = max(self.n, 1)
        return self.n

    def w(self, eng, n=None):
        eng.wait_ge(self.h, self.n if n is None else n)


class KB:

    def __init__(self, es, upto=None, dump=()):
        self.es = es
        self.upto = upto
        self.dump = list(dump)
        nc = bass.Bass("TRN2", target_bir_lowering=False)
        self.nc = nc
        self.stopped = False

        def ein(name, shape, dt=F32):
            return nc.dram_tensor(name, shape, dt, kind="ExternalInput").ap()

        self.X = ein("X", [D, TL])
        self.WqA = ein("WqA", [D, HL * DH])
        self.WkA = ein("WkA", [D, HL * DH])
        self.WvA = ein("WvA", [D, HL * DH])
        self.WoA = ein("WoA", [HL * DH, D])
        self.WqB = ein("WqB", [D, HL * DH])
        self.WkB = ein("WkB", [D, HL * DH])
        self.WvB = ein("WvB", [D, HL * DH])
        self.WoB = ein("WoB", [HL * DH, D])
        self.WguT = ein("WguT", [D, 2048])
        self.WpT = ein("WpT", [HID, 256])
        self.LQ1 = ein("LQ1", [128, 1])
        self.LK1 = ein("LK1", [128, 1])
        self.LQ2 = ein("LQ2", [128, 1])
        self.LK2 = ein("LK2", [128, 1])
        self.GAM = ein("GAM", [128, HL])
        self.BET = ein("BET", [128, HL])
        self.ONES1 = ein("ONES1", [1, 128])
        self.ONESC = ein("ONESC", [128, 1])
        self.OH2 = ein("OH2", [128, 2])
        self.ONES1_64 = ein("ONES1_64", [1, 64])
        self.GSEL = ein("GSEL", [128, 33])
        self.MASKS = ein("MASKS", [128, 4 * SW])

        self.OUT = nc.dram_tensor("OUT", [D, TL], F32, kind="ExternalOutput").ap()

        def din(name, shape, dt=F32, addr_space=None):
            if addr_space:
                return nc.dram_tensor(name, shape, dt, addr_space=addr_space).ap()
            return nc.dram_tensor(name, shape, dt).ap()

        self.xres = din("xres", [D, TL])
        self.ybuf = din("ybuf", [D, TL])
        self.yall = din("yall", [2 * D, TL], addr_space="Local")
        self.qbuf = din("qbuf", [HL * DH, T])
        self.kbuf = din("kbuf", [HL * DH, T])
        self.vbuf = din("vbuf", [T, HL * DH])
        self.acat = din("acat", [HL * DH, T])
        self.opart = din("opart", [2 * D, TL])
        self.rsout = din("rsout", [D, TL], addr_space="Local")
        self.brout = din("brout", [D, TL])
        self.hbuf1 = din("hbuf1", [HID, TL], FP16)
        self.hbuf2 = din("hbuf2", [HID, TL], FP16)
        self.wc_gu = din("wc_gu", [D, 2048], FP16)
        self.wg_gu = din("wg_gu", [8 * D, 2048], FP16, addr_space="Shared")
        self.wc_pr = din("wc_pr", [HID, 256], FP16)
        self.wg_pr = din("wg_pr", [8 * HID, 256], FP16, addr_space="Shared")

        self.mm = Sem(nc, es, "mm")
        self.ev = Sem(nc, es, "ev")
        self.vf = Sem(nc, es, "vf")
        self.pf = Sem(nc, es, "pf")
        self.bar = Sem(nc, es, "bar")
        self.ccs = [Sem(nc, es, f"cc{i}") for i in range(8)]
        self.ncc = 0
        self.dmasems = []

        def dsem(name):
            x = Sem(nc, es, name)
            self.dmasems.append(x)
            return x

        self.ld = dsem("ld")
        self.st = dsem("st")
        self.ld2 = [dsem(f"ld2_{i}") for i in range(2)]
        self.st2 = [dsem(f"st2_{i}") for i in range(2)]
        self.st4 = [dsem(f"st4_{i}") for i in range(4)]
        self.wld = dsem("wld")
        self.aq = dsem("aq")
        self.ak = [dsem(f"ak{i}") for i in range(2)]
        self.av = [dsem(f"av{i}") for i in range(2)]

        self.blk = es.enter_context(nc.Block())

        self.c_ones1 = self.sb("c_ones1", [1, 128])
        self.c_onesc = self.sb("c_onesc", [128, 1])
        self.c_oh2 = self.sb("c_oh2", [128, 2])
        self.c_ones1_64 = self.sb("c_ones1_64", [1, 64])
        self.c_gsel = self.sb("c_gsel", [128, 33])
        self.c_masks = self.sb("c_masks", [128, 4 * SW])
        self.c_lam = self.sb("c_lam", [128, 1])
        self.c_gam = self.sb("c_gam", [128, HL])
        self.c_bet = self.sb("c_bet", [128, HL])
        self.c_inv = self.sb("c_inv", [1, NSL * SW])
        self.c_eps_ln = self.sb("c_eps_ln", [1, 1])
        self.c_eps_rms = self.sb("c_eps_rms", [1, 1])

    _uid = 0

    def _u(self, name):
        KB._uid += 1
        return f"{name}_{KB._uid}"

    def sb(self, name, shape, dt=F32, es=None):
        return (es or self.es).enter_context(
            self.nc.sbuf_tensor(self._u(name), shape, dt))

    def ps(self, name, shape, dt=F32, es=None):
        return (es or self.es).enter_context(
            self.nc.psum_tensor(self._u(name), shape, dt))

    @property
    def sy(self):
        return self.nc.sync

    @property
    def te(self):
        return self.nc.tensor

    @property
    def ve(self):
        return self.nc.vector

    @property
    def se(self):
        return self.nc.scalar

    @property
    def ge(self):
        return self.nc.gpsimd

    def engines(self):
        return [self.sy, self.te, self.ve, self.se, self.ge]

    def barrier(self):
        for e in self.engines():
            self.bar.inc(e.nop(), 1)
        for e in self.engines():
            self.bar.w(e)

    def drain_barrier(self):
        self.barrier()
        for x in self.dmasems:
            x.w(self.sy)
        self.barrier()

    def phase(self, name):
        if self.stopped:
            return False
        self.barrier()
        if self.upto is not None and name == self.upto:
            pass
        return True

    def done_phase(self, name):
        if self.upto is not None and name == self.upto:
            self.stopped = True

    def ph_const(self):
        ld, ve, se, ge = self.ld, self.ve, self.se, self.ge
        self.vf.inc(ve.memset(self.c_eps_ln[:], float(LN_EPS)))
        self.vf.inc(ve.memset(self.c_eps_rms[:], float(RMS_EPS)))
        for dst, src in [(self.c_ones1, self.ONES1), (self.c_onesc, self.ONESC), (self.c_oh2, self.OH2),
                         (self.c_ones1_64, self.ONES1_64), (self.c_gsel, self.GSEL),
                         (self.c_masks, self.MASKS)]:
            self.ld.inc(self.sy.dma_start(out=dst[:], in_=src), 16)
        with ExitStack() as p:
            l1 = self.sb("lam_q1", [128, 1], es=p)
            l2 = self.sb("lam_k1", [128, 1], es=p)
            l3 = self.sb("lam_q2", [128, 1], es=p)
            l4 = self.sb("lam_k2", [128, 1], es=p)
            e1 = self.sb("lam_e1", [128, 1], es=p)
            e2 = self.sb("lam_e2", [128, 1], es=p)
            for dst, src in [(l1, self.LQ1), (l2, self.LK1), (l3, self.LQ2),
                             (l4, self.LK2), (self.c_gam, self.GAM), (self.c_bet, self.BET)]:
                self.ld.inc(self.sy.dma_start(out=dst[:], in_=src), 16)
            self.ld.w(ve)
            self.ld.w(se)
            self.ld.w(ge)
            self.vf.inc(ve.tensor_tensor(e1[:], l1[:], l2[:], ALU.mult))
            self.vf.inc(ve.tensor_tensor(e2[:], l3[:], l4[:], ALU.mult))
            self.vf.w(se)
            self.ev.inc(se.activation(e1[:], e1[:], AF.Exp))
            self.ev.inc(se.activation(e2[:], e2[:], AF.Exp))
            self.ev.w(ge)
            self.pf.inc(ge.tensor_tensor(self.c_lam[:], e1[:], e2[:], ALU.subtract))
            self.pf.w(ve)
            self.vf.inc(ve.tensor_scalar(self.c_lam[:], self.c_lam[:], float(LAM_INIT), None, ALU.add))
            self.vf.w(ve)
            self.vf.inc(ve.tensor_scalar(self.c_gam[:], self.c_gam[:], ONE_MINUS_LAM, None, ALU.mult))
            self.vf.inc(ve.tensor_scalar(self.c_bet[:], self.c_bet[:], ONE_MINUS_LAM, None, ALU.mult))

    def _quant_slab(self, wsrc, wdst, nrows, ncols):
        nc, sy, te, ve, se, ge = self.nc, self.sy, self.te, self.ve, self.se, self.ge
        ncs = (ncols + SW - 1) // SW
        with ExitStack() as p:
            w = [self.sb(f"qz_w{i}", [128, ncols], es=p) for i in range(2)]
            wb = self.sb("qz_wb", [128, SW], BF16, es=p)
            aw = self.sb("qz_aw", [128, SW], es=p)
            sc_b = self.sb("qz_scb", [33, SW], BF16, es=p)
            sc_f = self.sb("qz_scf", [33, SW], es=p)
            r1 = self.sb("qz_r1", [1, SW], es=p)
            rA = self.sb("qz_rA", [1, SW], es=p)
            rB = self.sb("qz_rB", [1, SW], es=p)
            m1 = self.sb("qz_m1", [128, SW], es=p)
            sg = self.sb("qz_sg", [128, SW], es=p)
            wh = self.sb("qz_wh", [128, SW], es=p)
            u = self.sb("qz_u", [128, SW], es=p)
            tdb = self.sb("qz_tdb", [128, SW], BF16, es=p)
            tob = self.sb("qz_tob", [128, SW], BF16, es=p)
            w16 = self.sb("qz_w16", [128, SW], FP16, es=p)
            pg = self.ps("qz_pg", [33, SW], es=p)
            prb = self.ps("qz_prb", [128, SW], es=p)
            psb = self.ps("qz_psb", [128, SW], es=p)
            nch = nrows // 128
            for ch in range(nch):
                buf = w[ch % 2]
                lsl = self.ld2[ch % 2]
                lsl.inc(sy.dma_start(out=buf[:], in_=wsrc[128 * ch:128 * (ch + 1), :]), 16)
                lsl.w(te)
                lsl.w(ve)
                for cs in range(ncs):
                    c0, c1 = cs * SW, min((cs + 1) * SW, ncols)
                    n = c1 - c0
                    self.st.w(ve)
                    self.vf.inc(ve.tensor_copy(wb[:, :n], buf[:, c0:c1]))
                    self.vf.w(se)
                    self.ev.inc(se.activation(aw[:, :n], wb[:, :n], AF.Abs))
                    self.ev.w(te)
                    self.vf.w(te)
                    self.mm.inc(te.matmul(pg[:, :n], self.c_gsel[:], aw[:, :n],
                                          start=True, stop=True))
                    self.mm.w(ve)
                    for base in (0, 32):
                        self.vf.inc(ve.tensor_scalar(
                            sc_f[base:base + 1, :n], pg[base:base + 1, :n],
                            1.0 / 64.0, 1e-8, ALU.mult, ALU.max))
                    self.vf.w(ge)
                    for base in (0, 32):
                        self.pf.inc(ge.tensor_copy(sc_b[base:base + 1, :n],
                                                   sc_f[base:base + 1, :n]))
                    self.pf.w(ve)
                    self.vf.inc(ve.tensor_copy(sc_f[0:1, :n], sc_b[0:1, :n]))
                    self.vf.inc(ve.tensor_copy(sc_f[32:33, :n], sc_b[32:33, :n]))
                    self.vf.w(ge)
                    self.pf.inc(ge.tensor_copy(r1[:, :n], sc_f[32:33, :n]))
                    self.pf.w(ve)
                    self.vf.inc(ve.reciprocal(rA[:, :n], sc_f[0:1, :n]))
                    self.vf.inc(ve.reciprocal(rB[:, :n], r1[:, :n]))
                    self.vf.w(te)
                    self.mm.inc(te.matmul(prb[0:64, :n], self.c_ones1_64[:], rA[:, :n],
                                          start=True, stop=True))
                    self.mm.inc(te.matmul(prb[64:128, :n], self.c_ones1_64[:], rB[:, :n],
                                          start=True, stop=True, tile_position=(0, 64)))
                    self.mm.inc(te.matmul(psb[0:64, :n], self.c_ones1_64[:], sc_f[0:1, :n],
                                          start=True, stop=True))
                    self.mm.inc(te.matmul(psb[64:128, :n], self.c_ones1_64[:], r1[:, :n],
                                          start=True, stop=True, tile_position=(0, 64)))
                    self.mm.w(ve)
                    self.vf.inc(ve.tensor_tensor(u[:, :n], wb[:, :n], prb[:, :n], ALU.mult))
                    self.vf.w(se)
                    self.ev.inc(se.activation(sg[:, :n], u[:, :n], AF.Sign))
                    self.ev.inc(se.activation(aw[:, :n], u[:, :n], AF.Abs))
                    self.ev.w(ve)
                    self.vf.inc(ve.tensor_scalar(m1[:, :n], aw[:, :n], float(QTHR),
                                                 None, ALU.is_gt))
                    self.vf.w(ve)
                    self.vf.inc(ve.tensor_tensor(wh[:, :n], m1[:, :n], sg[:, :n], ALU.mult))
                    self.vf.w(ve)
                    self.vf.inc(ve.tensor_tensor(wh[:, :n], wh[:, :n], psb[:, :n], ALU.mult))
                    self.vf.w(ve)
                    self.vf.inc(ve.tensor_tensor(tdb[:, :n], wh[:, :n], wb[:, :n],
                                                 ALU.subtract))
                    self.vf.w(ge)
                    self.pf.inc(ge.tensor_tensor(tob[:, :n], wb[:, :n], tdb[:, :n],
                                                 ALU.add))
                    self.pf.w(ve)
                    last = ve.tensor_copy(w16[:, :n], tob[:, :n])
                    self.vf.inc(last)
                    self.vf.w(ge)
                    self.st.inc(ge.dma_start(out=wdst[128 * ch:128 * (ch + 1), c0:c1],
                                             in_=w16[:, :n]), 16)
                self.st.w(sy)

    def ph_quant(self):
        self._quant_slab(self.WguT, self.wc_gu, D, 2048)
        self._quant_slab(self.WpT, self.wc_pr, HID, 256)
        ge = self.ge
        self.st.w(ge)
        g8 = [[0, 1, 2, 3, 4, 5, 6, 7]]
        c1, c2 = self.ccs[self.ncc], self.ccs[self.ncc + 1]
        self.ncc += 2
        c1.incc(ge.collective_compute("AllGather", ALU.bypass, replica_groups=g8,
                                      ins=[self.wc_gu], outs=[self.wg_gu]))
        c2.incc(ge.collective_compute("AllGather", ALU.bypass, replica_groups=g8,
                                      ins=[self.wc_pr], outs=[self.wg_pr]))
        self.cc_gu, self.cc_pr = c1, c2

    def ph_ln(self, xsrc, addsrc, xdst, ydst):
        nc, sy, te, ve, se, ge = self.nc, self.sy, self.te, self.ve, self.se, self.ge
        with ExitStack() as p:
            xs = self.sb("ln_x", [128, NKD * SW], es=p)
            ads = self.sb("ln_add", [128, NKD * SW], es=p)
            sq = self.sb("ln_sq", [128, SW], es=p)
            rows = self.sb("ln_rows", [1, 4 * SW], es=p)
            ys = self.sb("ln_y", [128, SW], es=p)
            p_s1 = self.ps("ln_ps1", [1, SW], es=p)
            p_s2 = self.ps("ln_ps2", [1, SW], es=p)
            p_bA = self.ps("ln_bA", [128, SW], es=p)
            p_bB = self.ps("ln_bB", [128, SW], es=p)
            for s in range(NSL):
                cols = slice(s * SW, (s + 1) * SW)
                lsl = self.ld2[s % 2]
                ssl = self.st2[s % 2]
                self.vf.w(sy)
                self.vf.w(te)
                for k in range(NKD):
                    lsl.inc(sy.dma_start(
                        out=xs[:, k * SW:(k + 1) * SW],
                        in_=xsrc[128 * k:128 * (k + 1), cols]), 16)
                    if addsrc is not None:
                        lsl.inc(sy.dma_start(
                            out=ads[:, k * SW:(k + 1) * SW],
                            in_=addsrc[128 * k:128 * (k + 1), cols]), 16)
                lsl.w(ve)
                if addsrc is not None:
                    for k in range(NKD):
                        ck = slice(k * SW, (k + 1) * SW)
                        inst = ve.tensor_tensor(xs[:, ck], xs[:, ck], ads[:, ck], ALU.add)
                        self.vf.inc(inst)
                    self.vf.w(ge)
                    if xdst is not None:
                        for k in range(NKD):
                            self.st.inc(ge.dma_start(
                                out=xdst[128 * k:128 * (k + 1), cols],
                                in_=xs[:, k * SW:(k + 1) * SW]), 16)
                    self.vf.w(se)
                    self.vf.w(te)
                lsl.w(se)
                lsl.w(te)
                for k in range(NKD):
                    ck = slice(k * SW, (k + 1) * SW)
                    self.mm.inc(te.matmul(p_s1[:], self.c_onesc[:], xs[:, ck],
                                          start=(k == 0), stop=(k == NKD - 1)))
                    if k > 0:
                        self.mm.w(se)
                    self.ev.inc(se.activation(sq[:], xs[:, ck], AF.Square))
                    self.ev.w(te)
                    self.mm.inc(te.matmul(p_s2[:], self.c_onesc[:], sq[:],
                                          start=(k == 0), stop=(k == NKD - 1)))
                self.mm.w(ve)
                R_M, R_E, R_R, R_MR = (rows[:, i * SW:(i + 1) * SW] for i in range(4))
                self.vf.inc(ve.tensor_scalar(R_M, p_s1[:], 1.0 / D, None, ALU.mult))
                self.vf.inc(ve.tensor_scalar(R_E, p_s2[:], 1.0 / D, None, ALU.mult))
                self.vf.w(ge)
                self.pf.inc(ge.tensor_tensor(R_MR, R_M, R_M, ALU.mult))
                self.pf.w(ve)
                self.vf.inc(ve.tensor_tensor(R_R, R_E, R_MR, ALU.subtract))
                self.vf.w(se)
                self.ev.inc(se.activation(R_R, R_R, AF.Sqrt, bias=self.c_eps_ln[:]))
                self.ev.w(ve)
                self.vf.inc(ve.reciprocal(R_R, R_R))
                self.vf.w(ge)
                self.pf.inc(ge.tensor_tensor(R_MR, R_M, R_R, ALU.mult))
                self.vf.w(te)
                self.pf.w(te)
                self.mm.inc(te.matmul(p_bA[:], self.c_ones1[:], R_R, start=True, stop=True))
                self.mm.inc(te.matmul(p_bB[:], self.c_ones1[:], R_MR, start=True, stop=True))
                self.mm.w(ve)
                for k in range(NKD):
                    ck = slice(k * SW, (k + 1) * SW)
                    ssl.w(ve)
                    self.vf.inc(ve.tensor_tensor(ys[:], xs[:, ck], p_bA[:], ALU.mult))
                    self.vf.w(ve)
                    inst = ve.tensor_tensor(ys[:], ys[:], p_bB[:], ALU.subtract)
                    self.vf.inc(inst)
                    self.vf.w(ge)
                    ssl.inc(ge.dma_start(out=ydst[128 * k:128 * (k + 1), cols],
                                         in_=ys[:]), 16)

    def ph_gather_y(self):
        ge = self.ge
        self.st.w(ge)
        cc = self.ccs[self.ncc]
        self.ncc += 1
        grp = [[0, 1], [2, 3], [4, 5], [6, 7]]
        cc.incc(ge.collective_compute("AllGather", ALU.bypass, replica_groups=grp,
                                      ins=[self.ybuf], outs=[self.yall]))
        cc.w(ge)
        self.cc_y = cc

    def ph_proj_fm(self, wsrc, src, dst, NKC, NOC, nstrip, src_row0=0,
                   w_resident=None):
        nc, sy, te, ve, se, ge = self.nc, self.sy, self.te, self.ve, self.se, self.ge
        with ExitStack() as p:
            wt = self.sb("pf_w", [128, NKC * NOC * 128], es=p) if w_resident is None else None
            rs = [self.sb(f"pf_r{i}", [128, NKC * SW], es=p) for i in range(2)]
            og = [self.sb(f"pf_o{i}", [128, SW], es=p) for i in range(4)]
            pp = [self.ps(f"pf_p{i}", [128, SW], es=p) for i in range(4)]
            if w_resident is None:
                for k in range(NKC):
                    self.wld.inc(sy.dma_start(
                        out=wt[:, k * NOC * 128:(k + 1) * NOC * 128],
                        in_=wsrc(k)), 16)

                def wtile(k, o):
                    return wt[:, k * NOC * 128 + o * 128: k * NOC * 128 + (o + 1) * 128]
            else:
                wtile = w_resident
            i = 0
            for s in range(nstrip):
                buf = rs[s % 2]
                lsl = self.ld2[s % 2]
                if s >= 2:
                    self.mm.w(sy, self.mm_strip_done[s - 2])
                for k in range(NKC):
                    lsl.inc(sy.dma_start(out=buf[:, k * SW:(k + 1) * SW],
                                         in_=src(k, s)), 16)
                if s == 0:
                    self.wld.w(te)
                lsl.w(te)
                for o in range(NOC):
                    bank = i % 4
                    ssl = self.st4[bank]
                    if i >= 4:
                        self.ev.w(te, self.ev_hist[i - 4])
                        ssl.w(se)
                    for k in range(NKC):
                        inst = te.matmul(pp[bank][:], wtile(k, o),
                                         buf[:, k * SW:(k + 1) * SW],
                                         start=(k == 0), stop=(k == NKC - 1))
                    self.mm.inc(inst)
                    self.mm.w(se)
                    self.ev_hist.append(self.ev.inc(se.copy(og[bank][:], pp[bank][:])))
                    self.ev.w(ge)
                    ssl.inc(ge.dma_start(out=dst(o, s), in_=og[bank][:]), 16)
                    i += 1
                self.mm_strip_done[s] = self.mm.n
            self.mm.w(sy)
            self.ev.w(sy)
            for x in self.st4:
                x.w(sy)

    def _init_proj_state(self):
        self.ev_hist = []
        self.st_hist = []
        self.mm_strip_done = {}

    def ph_proj_tm(self, wdram, src_ap, dst):
        nc, sy, te, ve, se, ge = self.nc, self.sy, self.te, self.ve, self.se, self.ge
        NOC = HL * DH // SW
        with ExitStack() as p:
            wt = self.sb("tm_w", [128, NKD * HL * DH], es=p)
            yt = [self.sb(f"tm_y{i}", [128, NKD * 128], es=p) for i in range(2)]
            og = [self.sb(f"tm_o{i}", [128, SW], es=p) for i in range(4)]
            pp = [self.ps(f"tm_p{i}", [128, SW], es=p) for i in range(4)]
            for k in range(NKD):
                self.wld.inc(sy.dma_start(out=wt[:, k * HL * DH:(k + 1) * HL * DH],
                                          in_=wdram[128 * k:128 * (k + 1), :]), 16)
            i = 0
            tb_done = {}
            for tb in range(T // 128):
                buf = yt[tb % 2]
                lsl = self.ld2[tb % 2]
                if tb >= 2:
                    self.mm.w(sy, tb_done[tb - 2])
                r = (tb * 128) // TL
                lt0 = (tb * 128) % TL
                for k in range(NKD):
                    lsl.inc(sy.dma_start(
                        out=buf[:, k * 128:(k + 1) * 128],
                        in_=src_ap[r * D + 128 * k: r * D + 128 * (k + 1),
                                   lt0:lt0 + 128]), 16)
                if tb == 0:
                    self.wld.w(te)
                lsl.w(te)
                for o in range(NOC):
                    bank = i % 4
                    ssl = self.st4[bank]
                    if i >= 4:
                        self.ev.w(te, self.ev_hist[i - 4])
                        ssl.w(se)
                    for k in range(NKD):
                        inst = te.matmul(pp[bank][:], buf[:, k * 128:(k + 1) * 128],
                                         wt[:, k * HL * DH + o * SW: k * HL * DH + (o + 1) * SW],
                                         start=(k == 0), stop=(k == NKD - 1))
                    self.mm.inc(inst)
                    self.mm.w(se)
                    self.ev_hist.append(self.ev.inc(se.copy(og[bank][:], pp[bank][:])))
                    self.ev.w(ge)
                    ssl.inc(ge.dma_start(
                        out=dst[tb * 128:(tb + 1) * 128, o * SW:(o + 1) * SW],
                        in_=og[bank][:]), 16)
                    i += 1
                tb_done[tb] = self.mm.n
            self.mm.w(sy)
            self.ev.w(sy)
            for x in self.st4:
                x.w(sy)

    def ph_attn_a(self):
        nc, sy, te, ve, se, ge = self.nc, self.sy, self.te, self.ve, self.se, self.ge
        with ExitStack() as p:
            qs = self.sb("aa_q", [128, SW], es=p)
            kt = [self.sb(f"aa_k{i}", [128, 512], es=p) for i in range(2)]
            vt = [self.sb(f"aa_v{i}", [128, 512], es=p) for i in range(2)]
            pt = [self.sb(f"aa_p{i}", [128, SW], es=p) for i in range(2)]
            sm = self.sb("aa_sm", [128, SW], es=p)
            ave = self.sb("aa_ave", [128, SW], es=p)
            oes = self.sb("aa_out", [128, SW], es=p)
            zrow = self.sb("aa_z", [1, SW], es=p)
            suf = self.sb("aa_suf", [128, 1], es=p)
            p_s = [self.ps(f"aa_ps{i}", [128, SW], es=p) for i in range(2)]
            p_av = self.ps("aa_pav", [128, SW], es=p)
            p_z = self.ps("aa_pz", [1, SW], es=p)
            p_vt = self.ps("aa_pvt", [128, 16], es=p)
            p_b = self.ps("aa_pb", [128, SW], es=p)
            for h in range(HL):
                self.vf.w(te)
                for seg in range(8):
                    buf = vt[seg % 2]
                    vsl = self.av[seg % 2]
                    self.mm.w(sy, self.aa_vld.get(seg % 2, 0))
                    vsl.inc(sy.dma_start(
                        out=buf[:].rearrange("p (c n) -> p c n", c=4),
                        in_=self.vbuf[seg * 512:(seg + 1) * 512,
                                      h * DH:(h + 1) * DH].rearrange(
                                          "(c p) n -> p c n", c=4)), 16)
                    vsl.w(te)
                    for j4 in range(4):
                        inst = te.matmul(p_vt[:, 2 * seg:2 * seg + 2],
                                         buf[:, j4 * 128:(j4 + 1) * 128],
                                         self.c_oh2[:],
                                         start=(j4 == 0), stop=(j4 == 3))
                    self.aa_vld[seg % 2] = self.mm.inc(inst)
                for s in range(NSG):
                    nch = 4 * (s + 1)
                    self.mm.w(sy)
                    self.aq.inc(sy.dma_start(
                        out=qs[:], in_=self.qbuf[h * DH:(h + 1) * DH,
                                                 s * SW:(s + 1) * SW]), 16)
                    self.mm.w(ve)
                    self.pf.w(ve)
                    if s < NSG - 1:
                        self.vf.inc(ve.reduce_sum(suf[:], p_vt[:, 2 * (s + 1):16], axis=AX.X))
                    else:
                        self.vf.inc(ve.memset(suf[:], 0.0))
                    for j in range(nch):
                        g, j4 = j // 4, j % 4
                        if j4 == 0:
                            self.mm.w(sy, self.aa_kld.get(g % 2, 0))
                            self.mm.w(sy, self.aa_vld.get(g % 2, 0))
                            ksl, vsl = self.ak[g % 2], self.av[g % 2]
                            ksl.inc(sy.dma_start(
                                out=kt[g % 2][:],
                                in_=self.kbuf[h * DH:(h + 1) * DH,
                                              g * 512:(g + 1) * 512]), 16)
                            vsl.inc(sy.dma_start(
                                out=vt[g % 2][:].rearrange("p (c n) -> p c n", c=4),
                                in_=self.vbuf[g * 512:(g + 1) * 512,
                                              h * DH:(h + 1) * DH].rearrange(
                                                  "(c p) n -> p c n", c=4)), 16)
                        kb_ = kt[g % 2][:, j4 * 128:(j4 + 1) * 128]
                        vb_ = vt[g % 2][:, j4 * 128:(j4 + 1) * 128]
                        if j == 0:
                            self.aq.w(te)
                            self.ev.w(te)
                            self.vf.w(te)
                        if j4 == 0:
                            self.ak[g % 2].w(te)
                            self.av[g % 2].w(te)
                        sbank = p_s[j % 2]
                        self.ev.w(te, self.aa_pex.get(j % 2, 0))
                        self.vf.w(te, self.aa_smv.get(j % 2, 0))
                        sc_i = te.matmul(sbank[:], kb_, qs[:], start=True, stop=True)
                        self.mm.inc(sc_i)
                        d = j - (nch - 4)
                        self.mm.w(se, self.aa_pav.get(j % 2, 0))
                        if d >= 0:
                            self.mm.w(ve)
                            self.ev.w(ve)
                            self.vf.inc(ve.tensor_tensor(
                                sm[:], sbank[:],
                                self.c_masks[:, d * SW:(d + 1) * SW], ALU.mult))
                            self.aa_smv[j % 2] = self.vf.n
                            self.vf.w(se)
                            pin = sm
                        else:
                            self.mm.w(se)
                            pin = sbank
                        pb = pt[j % 2]
                        e_i = se.activation(pb[:], pin[:], AF.Exp, scale=1.0 / 64.0)
                        self.aa_pex[j % 2] = self.ev.inc(e_i)
                        self.ev.w(te)
                        self.mm.inc(te.matmul(p_z[:], self.c_onesc[:], pb[:],
                                              start=(j == 0), stop=(j == nch - 1)))
                        inst = te.matmul(p_av[:], vb_, pb[:],
                                         start=(j == 0), stop=(j == nch - 1))
                        self.aa_pav[j % 2] = self.mm.inc(inst)
                        if j4 == 3 or j == nch - 1:
                            self.aa_kld[g % 2] = self.mm.n
                            self.aa_vld[g % 2] = self.mm.n
                    self.mm.w(se)
                    self.st.w(se)
                    self.ev.inc(se.copy(ave[:], p_av[:]))
                    self.ev.inc(se.copy(zrow[:], p_z[:]))
                    self.ev.w(ve)
                    nfut = float(T - 128 * nch)
                    self.vf.inc(ve.tensor_scalar(ave[:], ave[:], suf[:], None, ALU.add))
                    self.vf.inc(ve.tensor_scalar(zrow[:], zrow[:], nfut, None, ALU.add))
                    self.st.w(ve)
                    self.vf.w(ve)
                    self.vf.inc(ve.reciprocal(zrow[:], zrow[:]))
                    self.vf.w(te)
                    self.mm.inc(te.matmul(p_b[:], self.c_ones1[:], zrow[:],
                                          start=True, stop=True))
                    self.mm.w(ve)
                    self.vf.inc(ve.tensor_tensor(oes[:], ave[:], p_b[:], ALU.mult))
                    self.vf.w(ge)
                    self.st.inc(ge.dma_start(
                        out=self.acat[h * DH:(h + 1) * DH, s * SW:(s + 1) * SW],
                        in_=oes[:]), 16)

    def ph_attn_b(self):
        nc, sy, te, ve, se, ge = self.nc, self.sy, self.te, self.ve, self.se, self.ge
        with ExitStack() as p:
            qs = self.sb("ab_q", [128, SW], es=p)
            kt = [self.sb(f"ab_k{i}", [128, 512], es=p) for i in range(2)]
            vt = [self.sb(f"ab_v{i}", [128, 512], es=p) for i in range(2)]
            p1 = [self.sb(f"ab_p1{i}", [128, SW], es=p) for i in range(2)]
            p2 = [self.sb(f"ab_p2{i}", [128, SW], es=p) for i in range(2)]
            ev1 = self.sb("ab_ev1", [128, SW], es=p)
            ev2 = self.sb("ab_ev2", [128, SW], es=p)
            asq = self.sb("ab_asq", [128, SW], es=p)
            t1 = self.sb("ab_t1", [128, SW], es=p)
            t2 = self.sb("ab_t2", [128, SW], es=p)
            At = self.sb("ab_A", [128, SW], es=p)
            o1 = self.sb("ab_o1", [128, SW], es=p)
            rows = self.sb("ab_rows", [1, 6 * SW], es=p)
            p_s1 = [self.ps(f"ab_ps1{i}", [128, SW], es=p) for i in range(2)]
            p_s2 = [self.ps(f"ab_ps2{i}", [128, SW], es=p) for i in range(2)]
            p_a1 = self.ps("ab_pa1", [128, SW], es=p)
            p_a2 = self.ps("ab_pa2", [128, SW], es=p)
            p_z = self.ps("ab_pz", [33, SW], es=p)
            R_Z1, R_Z2, R_MU, R_E2, R_R, R_MR = (rows[:, i * SW:(i + 1) * SW]
                                                 for i in range(6))
            for h in range(HL):
                for s in range(NSG):
                    nch = 4 * (s + 1)
                    self.mm.w(sy)
                    self.aq.inc(sy.dma_start(
                        out=qs[:], in_=self.qbuf[h * DH:(h + 1) * DH,
                                                 s * SW:(s + 1) * SW]), 16)
                    for j in range(nch):
                        g, j4 = j // 4, j % 4
                        if j4 == 0:
                            self.mm.w(sy, self.ab_kld.get(g % 2, 0))
                            ksl, vsl = self.ak[g % 2], self.av[g % 2]
                            ksl.inc(sy.dma_start(
                                out=kt[g % 2][:],
                                in_=self.kbuf[h * DH:(h + 1) * DH,
                                              g * 512:(g + 1) * 512]), 16)
                            vsl.inc(sy.dma_start(
                                out=vt[g % 2][:].rearrange("p (c n) -> p c n", c=4),
                                in_=self.vbuf[g * 512:(g + 1) * 512,
                                              h * DH:(h + 1) * DH].rearrange(
                                                  "(c p) n -> p c n", c=4)), 16)
                        kb_ = kt[g % 2][:, j4 * 128:(j4 + 1) * 128]
                        vb_ = vt[g % 2][:, j4 * 128:(j4 + 1) * 128]
                        if j == 0:
                            self.aq.w(te)
                            self.ev.w(te)
                            self.vf.w(te)
                        if j4 == 0:
                            self.ak[g % 2].w(te)
                            self.av[g % 2].w(te)
                        self.ev.w(te, self.ab_pex.get(j % 2, 0))
                        self.vf.w(te, self.ab_pmv.get(j % 2, 0))
                        ps1, ps2 = p_s1[j % 2], p_s2[j % 2]
                        self.mm.inc(te.matmul(ps1[:], kb_[0:64, :], qs[0:64, :],
                                              start=True, stop=True,
                                              tile_position=(0, 0)))
                        self.mm.inc(te.matmul(ps2[:], kb_[64:128, :], qs[64:128, :],
                                              start=True, stop=True,
                                              tile_position=(64, 0)))
                        self.mm.w(se, self.ab_pav.get(j % 2, 0))
                        self.mm.w(se)
                        pb1, pb2 = p1[j % 2], p2[j % 2]
                        self.ev.inc(se.activation(pb1[:], ps1[:], AF.Exp, scale=0.125))
                        e2_i = se.activation(pb2[:], ps2[:], AF.Exp, scale=0.125)
                        self.ab_pex[j % 2] = self.ev.inc(e2_i)
                        d = j - (nch - 4)
                        if d >= 0:
                            self.ev.w(ve)
                            mk = self.c_masks[:, d * SW:(d + 1) * SW]
                            self.vf.inc(ve.tensor_tensor(pb1[:], pb1[:], mk, ALU.mult))
                            self.vf.inc(ve.tensor_tensor(pb2[:], pb2[:], mk, ALU.mult))
                            self.ab_pmv[j % 2] = self.vf.n
                            self.vf.w(te)
                        else:
                            self.ev.w(te)
                        self.mm.inc(te.matmul(p_z[0:1, :], self.c_onesc[:], pb1[:],
                                              start=(j == 0), stop=(j == nch - 1)))
                        self.mm.inc(te.matmul(p_z[32:33, :], self.c_onesc[:], pb2[:],
                                              start=(j == 0), stop=(j == nch - 1),
                                              tile_position=(0, 32)))
                        self.mm.inc(te.matmul(p_a1[:], vb_, pb1[:],
                                              start=(j == 0), stop=(j == nch - 1)))
                        inst = te.matmul(p_a2[:], vb_, pb2[:],
                                         start=(j == 0), stop=(j == nch - 1))
                        self.ab_pav[j % 2] = self.mm.inc(inst)
                        if j4 == 3 or j == nch - 1:
                            self.ab_kld[g % 2] = self.mm.n
                    p_b1, p_b2 = p_s1[0], p_s1[1]
                    self.mm.w(se)
                    self.st.w(se)
                    self.ev.inc(se.copy(ev1[:], p_a1[:]))
                    self.ev.inc(se.copy(ev2[:], p_a2[:]))
                    self.ev.inc(se.copy(R_Z1, p_z[0:1, :]))
                    self.ev.inc(se.copy(R_Z2, p_z[32:33, :]))
                    self.ev.w(ve)
                    self.st.w(ve)
                    self.vf.inc(ve.reciprocal(R_Z1, R_Z1))
                    self.vf.inc(ve.reciprocal(R_Z2, R_Z2))
                    self.vf.w(te)
                    self.ev.w(te)
                    self.mm.inc(te.matmul(p_b1[:], self.c_ones1[:], R_Z1,
                                          start=True, stop=True))
                    self.mm.inc(te.matmul(p_b2[:], self.c_ones1[:], R_Z2,
                                          start=True, stop=True))
                    self.mm.w(ve)
                    self.vf.inc(ve.tensor_tensor(t1[:], ev1[:], p_b1[:], ALU.mult))
                    self.vf.inc(ve.tensor_tensor(t2[:], ev2[:], p_b2[:], ALU.mult))
                    self.vf.w(ve)
                    self.vf.inc(ve.tensor_scalar(t2[:], t2[:], self.c_lam[:], None,
                                                 ALU.mult))
                    self.vf.w(ve)
                    A_i = ve.tensor_tensor(At[:], t1[:], t2[:], ALU.subtract)
                    self.vf.inc(A_i)
                    self.vf.w(se)
                    self.vf.w(te)
                    self.ev.inc(se.activation(asq[:], At[:], AF.Square))
                    self.ev.w(te)
                    self.mm.inc(te.matmul(p_z[0:1, :], self.c_onesc[:], At[:],
                                          start=True, stop=True))
                    self.mm.inc(te.matmul(p_z[32:33, :], self.c_onesc[:], asq[:],
                                          start=True, stop=True,
                                          tile_position=(0, 32)))
                    self.mm.w(ve)
                    self.vf.inc(ve.tensor_scalar(R_MU, p_z[0:1, :], 1.0 / 128.0,
                                                 None, ALU.mult))
                    self.vf.inc(ve.tensor_scalar(R_E2, p_z[32:33, :], 1.0 / 128.0,
                                                 None, ALU.mult))
                    self.vf.w(ge)
                    self.pf.inc(ge.tensor_tensor(R_MR, R_MU, R_MU, ALU.mult))
                    self.pf.w(ve)
                    self.vf.inc(ve.tensor_tensor(R_R, R_E2, R_MR, ALU.subtract))
                    self.vf.w(se)
                    self.ev.inc(se.activation(R_R, R_R, AF.Sqrt, bias=self.c_eps_ln[:]))
                    self.ev.w(ve)
                    self.vf.inc(ve.reciprocal(R_R, R_R))
                    self.vf.w(ge)
                    self.pf.inc(ge.tensor_tensor(R_MR, R_MU, R_R, ALU.mult))
                    self.vf.w(te)
                    self.pf.w(te)
                    self.mm.inc(te.matmul(p_b1[:], self.c_ones1[:], R_R,
                                          start=True, stop=True))
                    self.mm.inc(te.matmul(p_b2[:], self.c_ones1[:], R_MR,
                                          start=True, stop=True))
                    self.mm.w(ve)
                    self.vf.w(ve)
                    self.vf.inc(ve.tensor_tensor(o1[:], At[:], p_b1[:], ALU.mult))
                    self.vf.w(ve)
                    self.vf.inc(ve.tensor_tensor(o1[:], o1[:], p_b2[:], ALU.subtract))
                    self.vf.w(ve)
                    self.vf.inc(ve.tensor_scalar(o1[:], o1[:],
                                                 self.c_gam[:, h:h + 1],
                                                 self.c_bet[:, h:h + 1],
                                                 ALU.mult, ALU.add))
                    self.vf.w(ge)
                    self.st.inc(ge.dma_start(
                        out=self.acat[h * DH:(h + 1) * DH, s * SW:(s + 1) * SW],
                        in_=o1[:]), 16)

    def ph_oproj(self, wdram):
        NKC, NOC = HL * DH // 128, NKD
        self._init_proj_state()

        def wsrc(k):
            return wdram[128 * k:128 * (k + 1), :]

        def src(k, s):
            return self.acat[128 * k:128 * (k + 1), s * SW:(s + 1) * SW]

        def dst(o, s):
            half, ls = s // NSL, s % NSL
            return self.opart[half * D + o * 128: half * D + (o + 1) * 128,
                              ls * SW:(ls + 1) * SW]

        self.ph_proj_fm(wsrc, src, dst, NKC, NOC, NSG)

    def ph_rs(self):
        ge = self.ge
        self.st.w(ge)
        cc = self.ccs[self.ncc]
        self.ncc += 1
        grp = [[0, 1], [2, 3], [4, 5], [6, 7]]
        cc.incc(ge.collective_compute("ReduceScatter", ALU.add, replica_groups=grp,
                                      ins=[self.opart], outs=[self.rsout]))
        cc.w(ge)

    def ph_gu(self):
        nc, sy, te, ve, se, ge = self.nc, self.sy, self.te, self.ve, self.se, self.ge
        self.cc_gu.w(sy)
        self.cc_gu.w(ge)
        with ExitStack() as p:
            wtg = [self.sb(f"gu_wg{i}", [128, NKD * 128], FP16, es=p) for i in range(2)]
            wtu = [self.sb(f"gu_wu{i}", [128, NKD * 128], FP16, es=p) for i in range(2)]
            ysf = [self.sb(f"gu_yf{i}", [128, SW], es=p) for i in range(2)]
            ys1 = self.sb("gu_y1", [128, NKD * NSL * SW], FP16, es=p)
            ys2 = self.sb("gu_y2", [128, NKD * NSL * SW], FP16, es=p)
            sg = [self.sb(f"gu_sg{i}", [128, SW], es=p) for i in range(2)]
            hh = [self.sb(f"gu_h{i}", [128, SW], es=p) for i in range(NSL)]
            h16 = [self.sb(f"gu_h16{i}", [128, 2 * SW], FP16, es=p) for i in range(NSL)]
            hq = [self.sb(f"gu_hq{i}", [128, SW], es=p) for i in range(2)]
            p_g = [self.ps(f"gu_pg{i}", [128, SW], es=p) for i in range(2)]
            p_u = [self.ps(f"gu_pu{i}", [128, SW], es=p) for i in range(2)]
            p_ss = [self.ps(f"gu_ss{i}", [1, SW], es=p) for i in range(NSL)]
            ysplit_done = {}
            for s in range(NSL):
                for k in range(NKD):
                    par = k % 2
                    lsl = self.ld2[par]
                    sy.wait_ge(self.vf.h, ysplit_done.get(par, 0))
                    lsl.inc(sy.dma_start(
                        out=ysf[par][:],
                        in_=self.ybuf[128 * k:128 * (k + 1), s * SW:(s + 1) * SW]), 16)
                    lsl.w(ve)
                    c = slice((k * NSL + s) * SW, (k * NSL + s + 1) * SW)
                    self.vf.inc(ve.tensor_copy(ys1[:, c], ysf[par][:]))
                    self.vf.w(ve)
                    inst = ve.tensor_tensor(ys2[:, c], ysf[par][:], ys1[:, c],
                                            ALU.subtract)
                    ysplit_done[par] = self.vf.inc(inst)

            def yc1(k, s):
                return ys1[:, (k * NSL + s) * SW:(k * NSL + s + 1) * SW]

            def yc2(k, s):
                return ys2[:, (k * NSL + s) * SW:(k * NSL + s + 1) * SW]

            self.vf.w(te)
            hstate_ev = {}
            hstate_vf = {}
            for o in range(NH_CH):
                wg, wu = wtg[o % 2], wtu[o % 2]
                lsl = self.ld2[o % 2]
                if o >= 2:
                    self.mm.w(sy, self.gu_wdone[o - 2])
                for k in range(NKD):
                    gr, gc = (o * 128) // 2048, (o * 128) % 2048
                    ur, uc = ((HID + o * 128) // 2048, (HID + o * 128) % 2048)
                    lsl.inc(sy.dma_start(
                        out=wg[:, k * 128:(k + 1) * 128],
                        in_=self.wg_gu[gr * D + 128 * k: gr * D + 128 * (k + 1),
                                       gc:gc + 128]), 16)
                    lsl.inc(sy.dma_start(
                        out=wu[:, k * 128:(k + 1) * 128],
                        in_=self.wg_gu[ur * D + 128 * k: ur * D + 128 * (k + 1),
                                       uc:uc + 128]), 16)
                lsl.w(te)
                for s in range(NSL):
                    i = o * NSL + s
                    par = i % 2
                    self.ev.w(te, hstate_ev.get(par, 0))
                    self.vf.w(te, hstate_vf.get(par, 0))
                    for k in range(NKD):
                        wk = wg[:, k * 128:(k + 1) * 128]
                        te.matmul(p_g[par][:], wk, yc1(k, s), start=(k == 0), stop=False)
                        te.matmul(p_g[par][:], wk, yc2(k, s), start=False,
                                  stop=(k == NKD - 1))
                    for k in range(NKD):
                        wk = wu[:, k * 128:(k + 1) * 128]
                        te.matmul(p_u[par][:], wk, yc1(k, s), start=(k == 0), stop=False)
                        inst = te.matmul(p_u[par][:], wk, yc2(k, s), start=False,
                                         stop=(k == NKD - 1))
                    self.mm.inc(inst)
                    self.mm.w(se)
                    self.vf.w(se, hstate_vf.get(par, 0))
                    self.ev.inc(se.activation(sg[par][:], p_g[par][:], AF.Silu))
                    self.ev.w(ve)
                    hb = hh[s]
                    hsl = self.st4[s]
                    hsl.w(ve)
                    h_i = ve.tensor_tensor(hb[:], sg[par][:], p_u[par][:], ALU.mult)
                    self.vf.inc(h_i)
                    self.vf.w(ge)
                    self.pf.inc(ge.tensor_copy(h16[s][:, :SW], hb[:]))
                    self.pf.w(ve)
                    self.vf.inc(ve.tensor_tensor(h16[s][:, SW:], hb[:], h16[s][:, :SW],
                                                 ALU.subtract))
                    hstate_vf[par] = self.vf.n
                    self.vf.w(ge)
                    self.vf.w(se)
                    self.pf.w(se)
                    hsl.inc(ge.dma_start(
                        out=self.hbuf1[o * 128:(o + 1) * 128, s * SW:(s + 1) * SW],
                        in_=h16[s][:, :SW]), 16)
                    hsl.inc(ge.dma_start(
                        out=self.hbuf2[o * 128:(o + 1) * 128, s * SW:(s + 1) * SW],
                        in_=h16[s][:, SW:]), 16)
                    self.mm.w(se)
                    hstate_ev[par] = self.ev.inc(se.activation(hq[par][:], hb[:], AF.Square))
                    self.ev.w(te)
                    inst = te.matmul(p_ss[s][:], self.c_onesc[:], hq[par][:],
                                     start=(o == 0), stop=(o == NH_CH - 1))
                    self.mm.inc(inst)
                self.gu_wdone[o] = self.mm.n
            self.mm.w(se)
            with ExitStack() as p2:
                r0 = self.sb("gu_r0", [1, NSL * SW], es=p2)
                for s in range(NSL):
                    self.ev.inc(se.copy(r0[:, s * SW:(s + 1) * SW], p_ss[s][:]))
                self.ev.w(ve)
                self.vf.inc(ve.tensor_scalar(r0[:], r0[:], 1.0 / HID, None, ALU.mult))
                self.vf.w(se)
                self.ev.inc(se.activation(r0[:], r0[:], AF.Sqrt, bias=self.c_eps_rms[:]))
                self.ev.w(ve)
                self.vf.inc(ve.reciprocal(self.c_inv[:], r0[:]))
                self.vf.w(se)
                self.vf.w(te)

    def ph_proj_ffn(self):
        nc, sy, te, ve, se, ge = self.nc, self.sy, self.te, self.ve, self.se, self.ge
        self.cc_pr.w(sy)
        with ExitStack() as p:
            wt4 = [self.sb(f"fp_w{i}", [128, NH_CH * 128], FP16, es=p) for i in range(4)]
            hs = [self.sb(f"fp_h{i}", [128, 2 * SW], FP16, es=p) for i in range(2)]
            og = [self.sb(f"fp_o{i}", [128, SW], es=p) for i in range(2)]
            ot = [self.sb(f"fp_ot{i}", [128, SW], es=p) for i in range(2)]
            pp = [self.ps(f"fp_p{i}", [128, SW], es=p) for i in range(4)]
            p_iv = [self.ps(f"fp_iv{i}", [128, SW], es=p) for i in range(NSL)]
            self.vf.w(te)
            for s in range(NSL):
                self.mm.inc(te.matmul(p_iv[s][:], self.c_ones1[:],
                                      self.c_inv[:, s * SW:(s + 1) * SW],
                                      start=True, stop=True))
            wdone = 0
            for ob in range(NKD // 4):
                self.mm.w(sy, wdone)
                for t in range(4):
                    o = 4 * ob + t
                    wsl = self.ld2[t % 2]
                    gr, gc = (o * 128) // 256, (o * 128) % 256
                    for kh in range(NH_CH):
                        wsl.inc(sy.dma_start(
                            out=wt4[t][:, kh * 128:(kh + 1) * 128],
                            in_=self.wg_pr[gr * HID + 128 * kh:
                                           gr * HID + 128 * (kh + 1),
                                           gc:gc + 128]), 16)
                for s in range(NSL):
                    self.ev.w(te)
                    if s == 0:
                        self.ld2[0].w(te)
                        self.ld2[1].w(te)
                    for kh in range(NH_CH):
                        hb = hs[kh % 2]
                        hsl = self.ak[kh % 2]
                        self.mm.w(sy, self.fp_hld.get(kh % 2, 0))
                        hsl.inc(sy.dma_start(
                            out=hb[:, :SW],
                            in_=self.hbuf1[kh * 128:(kh + 1) * 128,
                                           s * SW:(s + 1) * SW]), 16)
                        hsl.inc(sy.dma_start(
                            out=hb[:, SW:],
                            in_=self.hbuf2[kh * 128:(kh + 1) * 128,
                                           s * SW:(s + 1) * SW]), 16)
                        hsl.w(te)
                        for t in range(4):
                            wk = wt4[t][:, kh * 128:(kh + 1) * 128]
                            te.matmul(pp[t][:], wk, hb[:, :SW],
                                      start=(kh == 0), stop=False)
                            inst = te.matmul(pp[t][:], wk, hb[:, SW:],
                                             start=False, stop=(kh == NH_CH - 1))
                        self.fp_hld[kh % 2] = self.mm.inc(inst)
                    self.mm.w(se)
                    for t in range(4):
                        o = 4 * ob + t
                        bank = t % 2
                        self.vf.w(se)
                        self.ev.inc(se.copy(og[bank][:], pp[t][:]))
                        self.ev.w(ve)
                        ssl = self.st2[bank]
                        ssl.w(ve)
                        o_i = ve.tensor_tensor(ot[bank][:], og[bank][:],
                                               p_iv[s][:], ALU.mult)
                        self.vf.inc(o_i)
                        self.vf.w(ge)
                        ssl.inc(ge.dma_start(
                            out=self.brout[o * 128:(o + 1) * 128,
                                           s * SW:(s + 1) * SW],
                            in_=ot[bank][:]), 16)
                wdone = self.mm.n

    def ph_final(self):
        nc, sy, ve, ge = self.nc, self.sy, self.ve, self.ge
        with ExitStack() as p:
            xa = self.sb("fin_x", [128, SW], es=p)
            xb = self.sb("fin_b", [128, SW], es=p)
            for s in range(NSL):
                for k in range(NKD):
                    cols = slice(s * SW, (s + 1) * SW)
                    self.ld.inc(sy.dma_start(out=xa[:], in_=self.xres[128 * k:128 * (k + 1), cols]), 16)
                    self.ld.inc(sy.dma_start(out=xb[:], in_=self.brout[128 * k:128 * (k + 1), cols]), 16)
                    self.ld.w(ve)
                    inst = ve.tensor_tensor(xa[:], xa[:], xb[:], ALU.add)
                    self.vf.inc(inst)
                    self.vf.w(ge)
                    self.st.inc(ge.dma_start(out=self.OUT[128 * k:128 * (k + 1), cols],
                                             in_=xa[:]), 16)
                    self.st.w(sy)
        self.st.w(ge)

    def ph_dump(self):
        if not self.dump:
            return
        ge = self.ge
        self.barrier()
        self.st.w(ge)
        for c in self.ccs:
            c.w(ge)
        for name in self.dump:
            src = getattr(self, name)
            shp = list(src.shape)
            o = self.nc.dram_tensor("dbg_" + name, shp, src.dtype,
                                    kind="ExternalOutput").ap()
            self.st.inc(ge.dma_start(out=o, in_=src), 16)
        self.st.w(ge)

    def build(self):
        ph = []

        def run(name, fn, *a):
            if self.stopped:
                return
            self.drain_barrier()
            fn(*a)
            if self.upto is not None and name == self.upto:
                self.stopped = True

        self.aa_vld = {}
        self.aa_kld = {}
        self.aa_pav = {}
        self.ab_pav = {}
        self.aa_pex = {}
        self.aa_smv = {}
        self.ab_kld = {}
        self.ab_pex = {}
        self.ab_pmv = {}
        self.gu_wdone = {}
        self.fp_hld = {}

        run("const", self.ph_const)
        run("quant", self.ph_quant)
        run("ln1", self.ph_ln, self.X, None, None, self.ybuf)
        run("gather1", self.ph_gather_y)
        run("qkvA", self._qkv, self.WqA, self.WkA, self.WvA)
        run("attnA", self.ph_attn_a)
        run("oprojA", self.ph_oproj, self.WoA)
        run("rsA", self.ph_rs)
        run("ln2", self.ph_ln, self.X, self.rsout, self.xres, self.ybuf)
        run("gu1", self.ph_gu)
        run("fproj1", self.ph_proj_ffn)
        run("ln3", self.ph_ln, self.xres, self.brout, self.xres, self.ybuf)
        run("gather3", self.ph_gather_y)
        run("qkvB", self._qkv, self.WqB, self.WkB, self.WvB)
        run("attnB", self.ph_attn_b)
        run("oprojB", self.ph_oproj, self.WoB)
        run("rsB", self.ph_rs)
        run("ln4", self.ph_ln, self.xres, self.rsout, self.xres, self.ybuf)
        run("gu2", self.ph_gu)
        run("fproj2", self.ph_proj_ffn)
        run("final", self.ph_final)
        self.ph_dump()
        return self.nc

    def _qkv(self, Wq, Wk, Wv):
        for wdram, dstbuf in ((Wq, self.qbuf), (Wk, self.kbuf)):
            NKC, NOC = NKD, HL * DH // 128
            self._init_proj_state()

            def wsrc(k, wd=wdram):
                return wd[128 * k:128 * (k + 1), :]

            def src(k, s):
                half = s // NSL
                ls = s % NSL
                return self.yall[half * D + 128 * k: half * D + 128 * (k + 1),
                                 ls * SW:(ls + 1) * SW]

            def dst(o, s, db=dstbuf):
                return db[o * 128:(o + 1) * 128, s * SW:(s + 1) * SW]

            self.ph_proj_fm(wsrc, src, dst, NKC, NOC, NSG)
        self._init_proj_state()
        self.ph_proj_tm(Wv, self.yall, self.vbuf)


def _host_consts():
    ones1 = np.ones((1, 128), np.float32)
    oh2 = np.full((128, 2), 0.5, np.float32)
    onesc = np.ones((128, 1), np.float32)
    ones1_64 = np.ones((1, 64), np.float32)
    gsel = np.zeros((128, 33), np.float32)
    gsel[0:64, 0] = 1.0
    gsel[64:128, 32] = 1.0
    masks = np.zeros((128, 4 * SW), np.float32)
    kk = np.arange(128)[:, None]
    qq = np.arange(SW)[None, :]
    for d in range(4):
        masks[:, d * SW:(d + 1) * SW] = (kk <= (qq - 128 * d)).astype(np.float32)
    return ones1, onesc, oh2, ones1_64, gsel, masks


def _make_in_maps(inputs):
    x = np.asarray(inputs["L0_attn_a_x"], np.float32)
    WqA = np.asarray(inputs["L0_attn_a_Wq"], np.float32)
    WkA = np.asarray(inputs["L0_attn_a_Wk"], np.float32)
    WvA = np.asarray(inputs["L0_attn_a_Wv"], np.float32)
    WoA = np.asarray(inputs["L0_attn_a_Wo"], np.float32)
    Wqkv = np.asarray(inputs["L0_attn_b_W_qkv"], np.float32)
    WoB = np.asarray(inputs["L0_attn_b_W_o"], np.float32)
    WguT = np.ascontiguousarray(np.asarray(inputs["L0_ffn_w_gate_up"], np.float32).T)
    WpT = np.ascontiguousarray(np.asarray(inputs["L0_ffn_w_proj"], np.float32).T)
    lq1 = np.asarray(inputs["L0_attn_b_lambda_q1"], np.float32).reshape(-1, 1)
    lk1 = np.asarray(inputs["L0_attn_b_lambda_k1"], np.float32).reshape(-1, 1)
    lq2 = np.asarray(inputs["L0_attn_b_lambda_q2"], np.float32).reshape(-1, 1)
    lk2 = np.asarray(inputs["L0_attn_b_lambda_k2"], np.float32).reshape(-1, 1)
    gam = np.asarray(inputs["L0_attn_b_gn_gamma"], np.float32)
    bet = np.asarray(inputs["L0_attn_b_gn_beta"], np.float32)
    WqkvT = np.ascontiguousarray(Wqkv.T)
    WoBT = np.ascontiguousarray(WoB.T)
    ones1, onesc, oh2, ones1_64, gsel, masks = _host_consts()

    in_maps = []
    for c in range(8):
        b, half = c // 2, c % 2
        hs = slice(half * HL * DH, (half + 1) * HL * DH)
        m = dict(
            X=np.ascontiguousarray(x[b, half * TL:(half + 1) * TL, :].T),
            WqA=np.ascontiguousarray(WqA[:, hs]),
            WkA=np.ascontiguousarray(WkA[:, hs]),
            WvA=np.ascontiguousarray(WvA[:, hs]),
            WoA=np.ascontiguousarray(WoA[hs, :]),
            WqB=np.ascontiguousarray(WqkvT[:, half * HL * DH:(half + 1) * HL * DH]),
            WkB=np.ascontiguousarray(WqkvT[:, D + half * HL * DH: D + (half + 1) * HL * DH]),
            WvB=np.ascontiguousarray(WqkvT[:, 2 * D + half * HL * DH: 2 * D + (half + 1) * HL * DH]),
            WoB=np.ascontiguousarray(WoBT[hs, :]),
            WguT=np.ascontiguousarray(WguT[:, c * 2048:(c + 1) * 2048]),
            WpT=np.ascontiguousarray(WpT[:, c * 256:(c + 1) * 256]),
            LQ1=lq1, LK1=lk1, LQ2=lq2, LK2=lk2,
            GAM=np.ascontiguousarray(gam[half * HL:(half + 1) * HL, :].T),
            BET=np.ascontiguousarray(bet[half * HL:(half + 1) * HL, :].T),
            ONES1=ones1, ONESC=onesc, OH2=oh2, ONES1_64=ones1_64, GSEL=gsel, MASKS=masks,
        )
        in_maps.append(m)
    return in_maps


def build_nc(upto=None, dump=()):
    es = ExitStack()
    with es:
        kb = KB(es, upto=upto, dump=dump)
        nc = kb.build()
    return nc


def kernel(**inputs):
    in_maps = _make_in_maps(inputs)
    nc = build_nc()
    res = run_bass_kernel_spmd(nc, in_maps, core_ids=list(range(8)))
    out = np.empty((B, T, D), np.float32)
    for c in range(8):
        b, half = c // 2, c % 2
        out[b, half * TL:(half + 1) * TL, :] = np.asarray(res.results[c]["OUT"]).T
    return out


# version 05 (final)
# speedup vs baseline: 1.1135x; 1.1135x over previous
import numpy as np
from contextlib import ExitStack

import concourse.bass as bass
import concourse.mybir as mybir
from concourse.bass_utils import run_bass_kernel_spmd

F32 = mybir.dt.float32
BF16 = mybir.dt.bfloat16
FP16 = mybir.dt.float16
AF = mybir.ActivationFunctionType
ALU = mybir.AluOpType
AX = mybir.AxisListType

B, T, D = 4, 4096, 2048
TL = T // 2
HN, HL, DH = 16, 8, 128
NKD = D // 128
SW = 512
NSG = T // SW
NSL = TL // SW
GUH = 16384
HID = 8192
NH_CH = HID // 128
LN_EPS = 1e-5
RMS_EPS = float(np.finfo(np.float32).eps)
LAM_INIT = float(np.float32(0.8) - np.float32(0.6) * np.float32(np.exp(np.float32(-3.6))))
ONE_MINUS_LAM = float(np.float32(1.0) - np.float32(LAM_INIT))
QTHR = 0.5 + 2.0 ** -11


class Sem:
    def __init__(self, nc, es, name):
        self.h = es.enter_context(nc.semaphore(name))
        self.n = 0

    def inc(self, inst, k=1):
        inst.then_inc(self.h, k)
        self.n += k
        return self.n

    def incc(self, inst):
        inst.then_inc(self.h)
        self.n = max(self.n, 1)
        return self.n

    def w(self, eng, n=None):
        eng.wait_ge(self.h, self.n if n is None else n)


class KB:

    def __init__(self, es, upto=None, dump=()):
        self.es = es
        self.upto = upto
        self.dump = list(dump)
        nc = bass.Bass("TRN2", target_bir_lowering=False)
        self.nc = nc
        self.stopped = False

        def ein(name, shape, dt=F32):
            return nc.dram_tensor(name, shape, dt, kind="ExternalInput").ap()

        self.X = ein("X", [D, TL])
        self.WqA = ein("WqA", [D, HL * DH])
        self.WkA = ein("WkA", [D, HL * DH])
        self.WvA = ein("WvA", [D, HL * DH])
        self.WoA = ein("WoA", [HL * DH, D])
        self.WqB = ein("WqB", [D, HL * DH])
        self.WkB = ein("WkB", [D, HL * DH])
        self.WvB = ein("WvB", [D, HL * DH])
        self.WoB = ein("WoB", [HL * DH, D])
        self.WguT = ein("WguT", [D, 2048])
        self.WpT = ein("WpT", [HID, 256])
        self.LQ1 = ein("LQ1", [128, 1])
        self.LK1 = ein("LK1", [128, 1])
        self.LQ2 = ein("LQ2", [128, 1])
        self.LK2 = ein("LK2", [128, 1])
        self.GAM = ein("GAM", [128, HL])
        self.BET = ein("BET", [128, HL])
        self.ONES1 = ein("ONES1", [1, 128])
        self.ONESC = ein("ONESC", [128, 1])
        self.OH2 = ein("OH2", [128, 2])
        self.ONES1_64 = ein("ONES1_64", [1, 64])
        self.GSEL = ein("GSEL", [128, 33])
        self.MASKS = ein("MASKS", [128, 4 * SW])

        self.OUT = nc.dram_tensor("OUT", [D, TL], F32, kind="ExternalOutput").ap()

        def din(name, shape, dt=F32, addr_space=None):
            if addr_space:
                return nc.dram_tensor(name, shape, dt, addr_space=addr_space).ap()
            return nc.dram_tensor(name, shape, dt).ap()

        self.xres = din("xres", [D, TL])
        self.ybuf = din("ybuf", [D, TL])
        self.yall = din("yall", [2 * D, TL], addr_space="Local")
        self.qbuf = din("qbuf", [HL * DH, T])
        self.kbuf = din("kbuf", [HL * DH, T])
        self.vbuf = din("vbuf", [T, HL * DH])
        self.acat = din("acat", [HL * DH, T])
        self.opart = din("opart", [2 * D, TL])
        self.rsout = din("rsout", [D, TL], addr_space="Local")
        self.brout = din("brout", [D, TL])
        self.hbuf1 = din("hbuf1", [HID, TL], FP16)
        self.hbuf2 = din("hbuf2", [HID, TL], FP16)
        self.wc_gu = din("wc_gu", [D, 2048], FP16)
        self.wg_gu = din("wg_gu", [8 * D, 2048], FP16, addr_space="Shared")
        self.wc_pr = din("wc_pr", [HID, 256], FP16)
        self.wg_pr = din("wg_pr", [8 * HID, 256], FP16, addr_space="Shared")

        self.mm = Sem(nc, es, "mm")
        self.ev = Sem(nc, es, "ev")
        self.vf = Sem(nc, es, "vf")
        self.pf = Sem(nc, es, "pf")
        self.bar = Sem(nc, es, "bar")
        self.ccs = [Sem(nc, es, f"cc{i}") for i in range(8)]
        self.ncc = 0
        self.dmasems = []

        def dsem(name):
            x = Sem(nc, es, name)
            self.dmasems.append(x)
            return x

        self.ld = dsem("ld")
        self.st = dsem("st")
        self.ld2 = [dsem(f"ld2_{i}") for i in range(2)]
        self.st2 = [dsem(f"st2_{i}") for i in range(2)]
        self.st4 = [dsem(f"st4_{i}") for i in range(4)]
        self.wld = dsem("wld")
        self.aq = dsem("aq")
        self.ak = [dsem(f"ak{i}") for i in range(2)]
        self.av = [dsem(f"av{i}") for i in range(2)]

        self.blk = es.enter_context(nc.Block())

        self.c_ones1 = self.sb("c_ones1", [1, 128])
        self.c_onesc = self.sb("c_onesc", [128, 1])
        self.c_oh2 = self.sb("c_oh2", [128, 2])
        self.c_ones1_64 = self.sb("c_ones1_64", [1, 64])
        self.c_gsel = self.sb("c_gsel", [128, 33])
        self.c_masks = self.sb("c_masks", [128, 4 * SW])
        self.c_lam = self.sb("c_lam", [128, 1])
        self.c_gam = self.sb("c_gam", [128, HL])
        self.c_bet = self.sb("c_bet", [128, HL])
        self.c_inv = self.sb("c_inv", [1, NSL * SW])
        self.c_eps_ln = self.sb("c_eps_ln", [1, 1])
        self.c_eps_rms = self.sb("c_eps_rms", [1, 1])

    _uid = 0

    def _u(self, name):
        KB._uid += 1
        return f"{name}_{KB._uid}"

    def sb(self, name, shape, dt=F32, es=None):
        return (es or self.es).enter_context(
            self.nc.sbuf_tensor(self._u(name), shape, dt))

    def ps(self, name, shape, dt=F32, es=None):
        return (es or self.es).enter_context(
            self.nc.psum_tensor(self._u(name), shape, dt))

    @property
    def sy(self):
        return self.nc.sync

    @property
    def te(self):
        return self.nc.tensor

    @property
    def ve(self):
        return self.nc.vector

    @property
    def se(self):
        return self.nc.scalar

    @property
    def ge(self):
        return self.nc.gpsimd

    def engines(self):
        return [self.sy, self.te, self.ve, self.se, self.ge]

    def barrier(self):
        for e in self.engines():
            self.bar.inc(e.nop(), 1)
        for e in self.engines():
            self.bar.w(e)

    def drain_barrier(self):
        self.barrier()
        for x in self.dmasems:
            x.w(self.sy)
        self.barrier()

    def phase(self, name):
        if self.stopped:
            return False
        self.barrier()
        if self.upto is not None and name == self.upto:
            pass
        return True

    def done_phase(self, name):
        if self.upto is not None and name == self.upto:
            self.stopped = True

    def ph_const(self):
        ld, ve, se, ge = self.ld, self.ve, self.se, self.ge
        self.vf.inc(ve.memset(self.c_eps_ln[:], float(LN_EPS)))
        self.vf.inc(ve.memset(self.c_eps_rms[:], float(RMS_EPS)))
        for dst, src in [(self.c_ones1, self.ONES1), (self.c_onesc, self.ONESC), (self.c_oh2, self.OH2),
                         (self.c_ones1_64, self.ONES1_64), (self.c_gsel, self.GSEL),
                         (self.c_masks, self.MASKS)]:
            self.ld.inc(self.sy.dma_start(out=dst[:], in_=src), 16)
        with ExitStack() as p:
            l1 = self.sb("lam_q1", [128, 1], es=p)
            l2 = self.sb("lam_k1", [128, 1], es=p)
            l3 = self.sb("lam_q2", [128, 1], es=p)
            l4 = self.sb("lam_k2", [128, 1], es=p)
            e1 = self.sb("lam_e1", [128, 1], es=p)
            e2 = self.sb("lam_e2", [128, 1], es=p)
            for dst, src in [(l1, self.LQ1), (l2, self.LK1), (l3, self.LQ2),
                             (l4, self.LK2), (self.c_gam, self.GAM), (self.c_bet, self.BET)]:
                self.ld.inc(self.sy.dma_start(out=dst[:], in_=src), 16)
            self.ld.w(ve)
            self.ld.w(se)
            self.ld.w(ge)
            self.vf.inc(ve.tensor_tensor(e1[:], l1[:], l2[:], ALU.mult))
            self.vf.inc(ve.tensor_tensor(e2[:], l3[:], l4[:], ALU.mult))
            self.vf.w(se)
            self.ev.inc(se.activation(e1[:], e1[:], AF.Exp))
            self.ev.inc(se.activation(e2[:], e2[:], AF.Exp))
            self.ev.w(ge)
            self.pf.inc(ge.tensor_tensor(self.c_lam[:], e1[:], e2[:], ALU.subtract))
            self.pf.w(ve)
            self.vf.inc(ve.tensor_scalar(self.c_lam[:], self.c_lam[:], float(LAM_INIT), None, ALU.add))
            self.vf.w(ve)
            self.vf.inc(ve.tensor_scalar(self.c_gam[:], self.c_gam[:], ONE_MINUS_LAM, None, ALU.mult))
            self.vf.inc(ve.tensor_scalar(self.c_bet[:], self.c_bet[:], ONE_MINUS_LAM, None, ALU.mult))

    def _quant_slab(self, wsrc, wdst, nrows, ncols):
        nc, sy, te, ve, se, ge = self.nc, self.sy, self.te, self.ve, self.se, self.ge
        ncs = (ncols + SW - 1) // SW
        with ExitStack() as p:
            w = [self.sb(f"qz_w{i}", [128, ncols], es=p) for i in range(2)]
            wb = self.sb("qz_wb", [128, SW], BF16, es=p)
            aw = self.sb("qz_aw", [128, SW], es=p)
            sc_b = self.sb("qz_scb", [33, SW], BF16, es=p)
            sc_f = self.sb("qz_scf", [33, SW], es=p)
            r1 = self.sb("qz_r1", [1, SW], es=p)
            rA = self.sb("qz_rA", [1, SW], es=p)
            rB = self.sb("qz_rB", [1, SW], es=p)
            m1 = self.sb("qz_m1", [128, SW], es=p)
            sg = self.sb("qz_sg", [128, SW], es=p)
            wh = self.sb("qz_wh", [128, SW], es=p)
            u = self.sb("qz_u", [128, SW], es=p)
            tdb = self.sb("qz_tdb", [128, SW], BF16, es=p)
            tob = self.sb("qz_tob", [128, SW], BF16, es=p)
            w16 = self.sb("qz_w16", [128, SW], FP16, es=p)
            pg = self.ps("qz_pg", [33, SW], es=p)
            prb = self.ps("qz_prb", [128, SW], es=p)
            psb = self.ps("qz_psb", [128, SW], es=p)
            nch = nrows // 128
            for ch in range(nch):
                buf = w[ch % 2]
                lsl = self.ld2[ch % 2]
                lsl.inc(sy.dma_start(out=buf[:], in_=wsrc[128 * ch:128 * (ch + 1), :]), 16)
                lsl.w(te)
                lsl.w(ve)
                for cs in range(ncs):
                    c0, c1 = cs * SW, min((cs + 1) * SW, ncols)
                    n = c1 - c0
                    self.st.w(ve)
                    self.vf.inc(ve.tensor_copy(wb[:, :n], buf[:, c0:c1]))
                    self.vf.w(se)
                    self.ev.inc(se.activation(aw[:, :n], wb[:, :n], AF.Abs))
                    self.ev.w(te)
                    self.vf.w(te)
                    self.mm.inc(te.matmul(pg[:, :n], self.c_gsel[:], aw[:, :n],
                                          start=True, stop=True))
                    self.mm.w(ve)
                    for base in (0, 32):
                        self.vf.inc(ve.tensor_scalar(
                            sc_f[base:base + 1, :n], pg[base:base + 1, :n],
                            1.0 / 64.0, 1e-8, ALU.mult, ALU.max))
                    self.vf.w(ge)
                    for base in (0, 32):
                        self.pf.inc(ge.tensor_copy(sc_b[base:base + 1, :n],
                                                   sc_f[base:base + 1, :n]))
                    self.pf.w(ve)
                    self.vf.inc(ve.tensor_copy(sc_f[0:1, :n], sc_b[0:1, :n]))
                    self.vf.inc(ve.tensor_copy(sc_f[32:33, :n], sc_b[32:33, :n]))
                    self.vf.w(ge)
                    self.pf.inc(ge.tensor_copy(r1[:, :n], sc_f[32:33, :n]))
                    self.pf.w(ve)
                    self.vf.inc(ve.reciprocal(rA[:, :n], sc_f[0:1, :n]))
                    self.vf.inc(ve.reciprocal(rB[:, :n], r1[:, :n]))
                    self.vf.w(te)
                    self.mm.inc(te.matmul(prb[0:64, :n], self.c_ones1_64[:], rA[:, :n],
                                          start=True, stop=True))
                    self.mm.inc(te.matmul(prb[64:128, :n], self.c_ones1_64[:], rB[:, :n],
                                          start=True, stop=True, tile_position=(0, 64)))
                    self.mm.inc(te.matmul(psb[0:64, :n], self.c_ones1_64[:], sc_f[0:1, :n],
                                          start=True, stop=True))
                    self.mm.inc(te.matmul(psb[64:128, :n], self.c_ones1_64[:], r1[:, :n],
                                          start=True, stop=True, tile_position=(0, 64)))
                    self.mm.w(ve)
                    self.vf.inc(ve.tensor_tensor(u[:, :n], wb[:, :n], prb[:, :n], ALU.mult))
                    self.vf.w(se)
                    self.ev.inc(se.activation(sg[:, :n], u[:, :n], AF.Sign))
                    self.ev.inc(se.activation(aw[:, :n], u[:, :n], AF.Abs))
                    self.ev.w(ve)
                    self.vf.inc(ve.tensor_scalar(m1[:, :n], aw[:, :n], float(QTHR),
                                                 None, ALU.is_gt))
                    self.vf.w(ve)
                    self.vf.inc(ve.tensor_tensor(wh[:, :n], m1[:, :n], sg[:, :n], ALU.mult))
                    self.vf.w(ve)
                    self.vf.inc(ve.tensor_tensor(wh[:, :n], wh[:, :n], psb[:, :n], ALU.mult))
                    self.vf.w(ve)
                    self.vf.inc(ve.tensor_tensor(tdb[:, :n], wh[:, :n], wb[:, :n],
                                                 ALU.subtract))
                    self.vf.w(ge)
                    self.pf.inc(ge.tensor_tensor(tob[:, :n], wb[:, :n], tdb[:, :n],
                                                 ALU.add))
                    self.pf.w(ve)
                    last = ve.tensor_copy(w16[:, :n], tob[:, :n])
                    self.vf.inc(last)
                    self.vf.w(ge)
                    self.st.inc(ge.dma_start(out=wdst[128 * ch:128 * (ch + 1), c0:c1],
                                             in_=w16[:, :n]), 16)
                self.st.w(sy)

    def ph_quant(self):
        self._quant_slab(self.WguT, self.wc_gu, D, 2048)
        self._quant_slab(self.WpT, self.wc_pr, HID, 256)
        ge = self.ge
        self.st.w(ge)
        g8 = [[0, 1, 2, 3, 4, 5, 6, 7]]
        c1, c2 = self.ccs[self.ncc], self.ccs[self.ncc + 1]
        self.ncc += 2
        c1.incc(ge.collective_compute("AllGather", ALU.bypass, replica_groups=g8,
                                      ins=[self.wc_gu], outs=[self.wg_gu]))
        c2.incc(ge.collective_compute("AllGather", ALU.bypass, replica_groups=g8,
                                      ins=[self.wc_pr], outs=[self.wg_pr]))
        self.cc_gu, self.cc_pr = c1, c2

    def ph_ln(self, xsrc, addsrc, xdst, ydst):
        nc, sy, te, ve, se, ge = self.nc, self.sy, self.te, self.ve, self.se, self.ge
        with ExitStack() as p:
            xs = self.sb("ln_x", [128, NKD * SW], es=p)
            ads = self.sb("ln_add", [128, NKD * SW], es=p)
            sq = self.sb("ln_sq", [128, SW], es=p)
            rows = self.sb("ln_rows", [1, 4 * SW], es=p)
            ys = self.sb("ln_y", [128, SW], es=p)
            p_s1 = self.ps("ln_ps1", [1, SW], es=p)
            p_s2 = self.ps("ln_ps2", [1, SW], es=p)
            p_bA = self.ps("ln_bA", [128, SW], es=p)
            p_bB = self.ps("ln_bB", [128, SW], es=p)
            for s in range(NSL):
                cols = slice(s * SW, (s + 1) * SW)
                lsl = self.ld2[s % 2]
                ssl = self.st2[s % 2]
                self.vf.w(sy)
                self.vf.w(te)
                for k in range(NKD):
                    lsl.inc(sy.dma_start(
                        out=xs[:, k * SW:(k + 1) * SW],
                        in_=xsrc[128 * k:128 * (k + 1), cols]), 16)
                    if addsrc is not None:
                        lsl.inc(sy.dma_start(
                            out=ads[:, k * SW:(k + 1) * SW],
                            in_=addsrc[128 * k:128 * (k + 1), cols]), 16)
                lsl.w(ve)
                if addsrc is not None:
                    for k in range(NKD):
                        ck = slice(k * SW, (k + 1) * SW)
                        inst = ve.tensor_tensor(xs[:, ck], xs[:, ck], ads[:, ck], ALU.add)
                        self.vf.inc(inst)
                    self.vf.w(ge)
                    if xdst is not None:
                        for k in range(NKD):
                            self.st.inc(ge.dma_start(
                                out=xdst[128 * k:128 * (k + 1), cols],
                                in_=xs[:, k * SW:(k + 1) * SW]), 16)
                    self.vf.w(se)
                    self.vf.w(te)
                lsl.w(se)
                lsl.w(te)
                for k in range(NKD):
                    ck = slice(k * SW, (k + 1) * SW)
                    self.mm.inc(te.matmul(p_s1[:], self.c_onesc[:], xs[:, ck],
                                          start=(k == 0), stop=(k == NKD - 1)))
                    if k > 0:
                        self.mm.w(se)
                    self.ev.inc(se.activation(sq[:], xs[:, ck], AF.Square))
                    self.ev.w(te)
                    self.mm.inc(te.matmul(p_s2[:], self.c_onesc[:], sq[:],
                                          start=(k == 0), stop=(k == NKD - 1)))
                self.mm.w(ve)
                R_M, R_E, R_R, R_MR = (rows[:, i * SW:(i + 1) * SW] for i in range(4))
                self.vf.inc(ve.tensor_scalar(R_M, p_s1[:], 1.0 / D, None, ALU.mult))
                self.vf.inc(ve.tensor_scalar(R_E, p_s2[:], 1.0 / D, None, ALU.mult))
                self.vf.w(ge)
                self.pf.inc(ge.tensor_tensor(R_MR, R_M, R_M, ALU.mult))
                self.pf.w(ve)
                self.vf.inc(ve.tensor_tensor(R_R, R_E, R_MR, ALU.subtract))
                self.vf.w(se)
                self.ev.inc(se.activation(R_R, R_R, AF.Sqrt, bias=self.c_eps_ln[:]))
                self.ev.w(ve)
                self.vf.inc(ve.reciprocal(R_R, R_R))
                self.vf.w(ge)
                self.pf.inc(ge.tensor_tensor(R_MR, R_M, R_R, ALU.mult))
                self.vf.w(te)
                self.pf.w(te)
                self.mm.inc(te.matmul(p_bA[:], self.c_ones1[:], R_R, start=True, stop=True))
                self.mm.inc(te.matmul(p_bB[:], self.c_ones1[:], R_MR, start=True, stop=True))
                self.mm.w(ve)
                for k in range(NKD):
                    ck = slice(k * SW, (k + 1) * SW)
                    ssl.w(ve)
                    self.vf.inc(ve.tensor_tensor(ys[:], xs[:, ck], p_bA[:], ALU.mult))
                    self.vf.w(ve)
                    inst = ve.tensor_tensor(ys[:], ys[:], p_bB[:], ALU.subtract)
                    self.vf.inc(inst)
                    self.vf.w(ge)
                    ssl.inc(ge.dma_start(out=ydst[128 * k:128 * (k + 1), cols],
                                         in_=ys[:]), 16)

    def ph_gather_y(self):
        ge = self.ge
        self.st.w(ge)
        cc = self.ccs[self.ncc]
        self.ncc += 1
        grp = [[0, 1], [2, 3], [4, 5], [6, 7]]
        cc.incc(ge.collective_compute("AllGather", ALU.bypass, replica_groups=grp,
                                      ins=[self.ybuf], outs=[self.yall]))
        cc.w(ge)
        self.cc_y = cc

    def ph_proj_fm(self, wsrc, src, dst, NKC, NOC, nstrip, src_row0=0,
                   w_resident=None):
        nc, sy, te, ve, se, ge = self.nc, self.sy, self.te, self.ve, self.se, self.ge
        with ExitStack() as p:
            wt = self.sb("pf_w", [128, NKC * NOC * 128], es=p) if w_resident is None else None
            rs = [self.sb(f"pf_r{i}", [128, NKC * SW], es=p) for i in range(2)]
            og = [self.sb(f"pf_o{i}", [128, SW], es=p) for i in range(4)]
            pp = [self.ps(f"pf_p{i}", [128, SW], es=p) for i in range(4)]
            if w_resident is None:
                for k in range(NKC):
                    self.wld.inc(sy.dma_start(
                        out=wt[:, k * NOC * 128:(k + 1) * NOC * 128],
                        in_=wsrc(k)), 16)

                def wtile(k, o):
                    return wt[:, k * NOC * 128 + o * 128: k * NOC * 128 + (o + 1) * 128]
            else:
                wtile = w_resident
            i = 0
            for s in range(nstrip):
                buf = rs[s % 2]
                lsl = self.ld2[s % 2]
                if s >= 2:
                    self.mm.w(sy, self.mm_strip_done[s - 2])
                for k in range(NKC):
                    lsl.inc(sy.dma_start(out=buf[:, k * SW:(k + 1) * SW],
                                         in_=src(k, s)), 16)
                if s == 0:
                    self.wld.w(te)
                lsl.w(te)
                for o in range(NOC):
                    bank = i % 4
                    ssl = self.st4[bank]
                    if i >= 4:
                        self.ev.w(te, self.ev_hist[i - 4])
                        ssl.w(se)
                    for k in range(NKC):
                        inst = te.matmul(pp[bank][:], wtile(k, o),
                                         buf[:, k * SW:(k + 1) * SW],
                                         start=(k == 0), stop=(k == NKC - 1))
                    self.mm.inc(inst)
                    self.mm.w(se)
                    self.ev_hist.append(self.ev.inc(se.copy(og[bank][:], pp[bank][:])))
                    self.ev.w(ge)
                    ssl.inc(ge.dma_start(out=dst(o, s), in_=og[bank][:]), 16)
                    i += 1
                self.mm_strip_done[s] = self.mm.n
            self.mm.w(sy)
            self.ev.w(sy)
            for x in self.st4:
                x.w(sy)

    def _init_proj_state(self):
        self.ev_hist = []
        self.st_hist = []
        self.mm_strip_done = {}

    def ph_proj_tm(self, wdram, src_ap, dst):
        nc, sy, te, ve, se, ge = self.nc, self.sy, self.te, self.ve, self.se, self.ge
        NOC = HL * DH // SW
        with ExitStack() as p:
            wt = self.sb("tm_w", [128, NKD * HL * DH], es=p)
            yt = [self.sb(f"tm_y{i}", [128, NKD * 128], es=p) for i in range(2)]
            og = [self.sb(f"tm_o{i}", [128, SW], es=p) for i in range(4)]
            pp = [self.ps(f"tm_p{i}", [128, SW], es=p) for i in range(4)]
            for k in range(NKD):
                self.wld.inc(sy.dma_start(out=wt[:, k * HL * DH:(k + 1) * HL * DH],
                                          in_=wdram[128 * k:128 * (k + 1), :]), 16)
            i = 0
            tb_done = {}
            for tb in range(T // 128):
                buf = yt[tb % 2]
                lsl = self.ld2[tb % 2]
                if tb >= 2:
                    self.mm.w(sy, tb_done[tb - 2])
                r = (tb * 128) // TL
                lt0 = (tb * 128) % TL
                for k in range(NKD):
                    lsl.inc(sy.dma_start(
                        out=buf[:, k * 128:(k + 1) * 128],
                        in_=src_ap[r * D + 128 * k: r * D + 128 * (k + 1),
                                   lt0:lt0 + 128]), 16)
                if tb == 0:
                    self.wld.w(te)
                lsl.w(te)
                for o in range(NOC):
                    bank = i % 4
                    ssl = self.st4[bank]
                    if i >= 4:
                        self.ev.w(te, self.ev_hist[i - 4])
                        ssl.w(se)
                    for k in range(NKD):
                        inst = te.matmul(pp[bank][:], buf[:, k * 128:(k + 1) * 128],
                                         wt[:, k * HL * DH + o * SW: k * HL * DH + (o + 1) * SW],
                                         start=(k == 0), stop=(k == NKD - 1))
                    self.mm.inc(inst)
                    self.mm.w(se)
                    self.ev_hist.append(self.ev.inc(se.copy(og[bank][:], pp[bank][:])))
                    self.ev.w(ge)
                    ssl.inc(ge.dma_start(
                        out=dst[tb * 128:(tb + 1) * 128, o * SW:(o + 1) * SW],
                        in_=og[bank][:]), 16)
                    i += 1
                tb_done[tb] = self.mm.n
            self.mm.w(sy)
            self.ev.w(sy)
            for x in self.st4:
                x.w(sy)

    def ph_attn_a(self):
        nc, sy, te, ve, se, ge = self.nc, self.sy, self.te, self.ve, self.se, self.ge
        with ExitStack() as p:
            qs = self.sb("aa_q", [128, SW], es=p)
            kt = [self.sb(f"aa_k{i}", [128, 512], es=p) for i in range(2)]
            vt = [self.sb(f"aa_v{i}", [128, 512], es=p) for i in range(2)]
            pt = [self.sb(f"aa_p{i}", [128, SW], es=p) for i in range(2)]
            sm = self.sb("aa_sm", [128, SW], es=p)
            ave = self.sb("aa_ave", [128, SW], es=p)
            oes = self.sb("aa_out", [128, SW], es=p)
            zrow = self.sb("aa_z", [1, SW], es=p)
            suf = self.sb("aa_suf", [128, 1], es=p)
            p_s = [self.ps(f"aa_ps{i}", [128, SW], es=p) for i in range(2)]
            p_av = self.ps("aa_pav", [128, SW], es=p)
            p_z = self.ps("aa_pz", [1, SW], es=p)
            p_vt = self.ps("aa_pvt", [128, 16], es=p)
            p_b = self.ps("aa_pb", [128, SW], es=p)
            for h in range(HL):
                self.vf.w(te)
                for seg in range(8):
                    buf = vt[seg % 2]
                    vsl = self.av[seg % 2]
                    self.mm.w(sy, self.aa_vld.get(seg % 2, 0))
                    vsl.inc(sy.dma_start(
                        out=buf[:].rearrange("p (c n) -> p c n", c=4),
                        in_=self.vbuf[seg * 512:(seg + 1) * 512,
                                      h * DH:(h + 1) * DH].rearrange(
                                          "(c p) n -> p c n", c=4)), 16)
                    vsl.w(te)
                    for j4 in range(4):
                        inst = te.matmul(p_vt[:, 2 * seg:2 * seg + 2],
                                         buf[:, j4 * 128:(j4 + 1) * 128],
                                         self.c_oh2[:],
                                         start=(j4 == 0), stop=(j4 == 3))
                    self.aa_vld[seg % 2] = self.mm.inc(inst)
                for s in range(NSG):
                    nch = 4 * (s + 1)
                    self.mm.w(sy)
                    self.aq.inc(sy.dma_start(
                        out=qs[:], in_=self.qbuf[h * DH:(h + 1) * DH,
                                                 s * SW:(s + 1) * SW]), 16)
                    self.mm.w(ve)
                    self.pf.w(ve)
                    if s < NSG - 1:
                        self.vf.inc(ve.reduce_sum(suf[:], p_vt[:, 2 * (s + 1):16], axis=AX.X))
                    else:
                        self.vf.inc(ve.memset(suf[:], 0.0))
                    def front_a(j):
                        g, j4 = j // 4, j % 4
                        if j4 == 0:
                            self.mm.w(sy, self.aa_kld.get(g % 2, 0))
                            self.mm.w(sy, self.aa_vld.get(g % 2, 0))
                            ksl, vsl = self.ak[g % 2], self.av[g % 2]
                            ksl.inc(sy.dma_start(
                                out=kt[g % 2][:],
                                in_=self.kbuf[h * DH:(h + 1) * DH,
                                              g * 512:(g + 1) * 512]), 16)
                            vsl.inc(sy.dma_start(
                                out=vt[g % 2][:].rearrange("p (c n) -> p c n", c=4),
                                in_=self.vbuf[g * 512:(g + 1) * 512,
                                              h * DH:(h + 1) * DH].rearrange(
                                                  "(c p) n -> p c n", c=4)), 16)
                        kb_ = kt[g % 2][:, j4 * 128:(j4 + 1) * 128]
                        if j == 0:
                            self.aq.w(te)
                            self.ev.w(te)
                            self.vf.w(te)
                        if j4 == 0:
                            self.ak[g % 2].w(te)
                            self.av[g % 2].w(te)
                        sbank = p_s[j % 2]
                        self.ev.w(te, self.aa_pex.get(j % 2, 0))
                        self.vf.w(te, self.aa_smv.get(j % 2, 0))
                        sc_i = te.matmul(sbank[:], kb_, qs[:], start=True, stop=True)
                        scn = self.mm.inc(sc_i)
                        d = j - (nch - 4)
                        self.mm.w(se, self.aa_pav.get(j % 2, 0))
                        if d >= 0:
                            self.mm.w(ve, scn)
                            self.ev.w(ve)
                            self.vf.inc(ve.tensor_tensor(
                                sm[:], sbank[:],
                                self.c_masks[:, d * SW:(d + 1) * SW], ALU.mult))
                            self.aa_smv[j % 2] = self.vf.n
                            self.vf.w(se)
                            pin = sm
                        else:
                            self.mm.w(se, scn)
                            pin = sbank
                        pb = pt[j % 2]
                        e_i = se.activation(pb[:], pin[:], AF.Exp, scale=1.0 / 64.0)
                        self.aa_pex[j % 2] = self.ev.inc(e_i)

                    def back_a(j):
                        g, j4 = j // 4, j % 4
                        vb_ = vt[g % 2][:, j4 * 128:(j4 + 1) * 128]
                        pb = pt[j % 2]
                        self.ev.w(te, self.aa_pex[j % 2])
                        self.mm.inc(te.matmul(p_z[:], self.c_onesc[:], pb[:],
                                              start=(j == 0), stop=(j == nch - 1)))
                        inst = te.matmul(p_av[:], vb_, pb[:],
                                         start=(j == 0), stop=(j == nch - 1))
                        self.aa_pav[j % 2] = self.mm.inc(inst)
                        if j4 == 3 or j == nch - 1:
                            self.aa_kld[g % 2] = self.mm.n
                            self.aa_vld[g % 2] = self.mm.n

                    for j in range(nch):
                        front_a(j)
                        if j > 0:
                            back_a(j - 1)
                    back_a(nch - 1)
                    self.mm.w(se)
                    self.st.w(se)
                    self.ev.inc(se.copy(ave[:], p_av[:]))
                    self.ev.inc(se.copy(zrow[:], p_z[:]))
                    self.ev.w(ve)
                    nfut = float(T - 128 * nch)
                    self.vf.inc(ve.tensor_scalar(ave[:], ave[:], suf[:], None, ALU.add))
                    self.vf.inc(ve.tensor_scalar(zrow[:], zrow[:], nfut, None, ALU.add))
                    self.st.w(ve)
                    self.vf.w(ve)
                    self.vf.inc(ve.reciprocal(zrow[:], zrow[:]))
                    self.vf.w(te)
                    self.mm.inc(te.matmul(p_b[:], self.c_ones1[:], zrow[:],
                                          start=True, stop=True))
                    self.mm.w(ve)
                    self.vf.inc(ve.tensor_tensor(oes[:], ave[:], p_b[:], ALU.mult))
                    self.vf.w(ge)
                    self.st.inc(ge.dma_start(
                        out=self.acat[h * DH:(h + 1) * DH, s * SW:(s + 1) * SW],
                        in_=oes[:]), 16)

    def ph_attn_b(self):
        nc, sy, te, ve, se, ge = self.nc, self.sy, self.te, self.ve, self.se, self.ge
        with ExitStack() as p:
            qs = self.sb("ab_q", [128, SW], es=p)
            kt = [self.sb(f"ab_k{i}", [128, 512], es=p) for i in range(2)]
            vt = [self.sb(f"ab_v{i}", [128, 512], es=p) for i in range(2)]
            p1 = [self.sb(f"ab_p1{i}", [128, SW], es=p) for i in range(2)]
            p2 = [self.sb(f"ab_p2{i}", [128, SW], es=p) for i in range(2)]
            ev1 = self.sb("ab_ev1", [128, SW], es=p)
            ev2 = self.sb("ab_ev2", [128, SW], es=p)
            asq = self.sb("ab_asq", [128, SW], es=p)
            t1 = self.sb("ab_t1", [128, SW], es=p)
            t2 = self.sb("ab_t2", [128, SW], es=p)
            At = self.sb("ab_A", [128, SW], es=p)
            o1 = self.sb("ab_o1", [128, SW], es=p)
            rows = self.sb("ab_rows", [1, 6 * SW], es=p)
            p_s1 = [self.ps(f"ab_ps1{i}", [128, SW], es=p) for i in range(2)]
            p_s2 = [self.ps(f"ab_ps2{i}", [128, SW], es=p) for i in range(2)]
            p_a1 = self.ps("ab_pa1", [128, SW], es=p)
            p_a2 = self.ps("ab_pa2", [128, SW], es=p)
            p_z = self.ps("ab_pz", [33, SW], es=p)
            R_Z1, R_Z2, R_MU, R_E2, R_R, R_MR = (rows[:, i * SW:(i + 1) * SW]
                                                 for i in range(6))
            for h in range(HL):
                for s in range(NSG):
                    nch = 4 * (s + 1)
                    self.mm.w(sy)
                    self.aq.inc(sy.dma_start(
                        out=qs[:], in_=self.qbuf[h * DH:(h + 1) * DH,
                                                 s * SW:(s + 1) * SW]), 16)
                    def front_b(j):
                        g, j4 = j // 4, j % 4
                        if j4 == 0:
                            self.mm.w(sy, self.ab_kld.get(g % 2, 0))
                            ksl, vsl = self.ak[g % 2], self.av[g % 2]
                            ksl.inc(sy.dma_start(
                                out=kt[g % 2][:],
                                in_=self.kbuf[h * DH:(h + 1) * DH,
                                              g * 512:(g + 1) * 512]), 16)
                            vsl.inc(sy.dma_start(
                                out=vt[g % 2][:].rearrange("p (c n) -> p c n", c=4),
                                in_=self.vbuf[g * 512:(g + 1) * 512,
                                              h * DH:(h + 1) * DH].rearrange(
                                                  "(c p) n -> p c n", c=4)), 16)
                        kb_ = kt[g % 2][:, j4 * 128:(j4 + 1) * 128]
                        if j == 0:
                            self.aq.w(te)
                            self.ev.w(te)
                            self.vf.w(te)
                        if j4 == 0:
                            self.ak[g % 2].w(te)
                            self.av[g % 2].w(te)
                        self.ev.w(te, self.ab_pex.get(j % 2, 0))
                        self.vf.w(te, self.ab_pmv.get(j % 2, 0))
                        ps1, ps2 = p_s1[j % 2], p_s2[j % 2]
                        self.mm.inc(te.matmul(ps1[:], kb_[0:64, :], qs[0:64, :],
                                              start=True, stop=True,
                                              tile_position=(0, 0)))
                        scn = self.mm.inc(te.matmul(ps2[:], kb_[64:128, :],
                                                    qs[64:128, :],
                                                    start=True, stop=True,
                                                    tile_position=(64, 0)))
                        self.mm.w(se, self.ab_pav.get(j % 2, 0))
                        self.mm.w(se, scn)
                        pb1, pb2 = p1[j % 2], p2[j % 2]
                        self.ev.inc(se.activation(pb1[:], ps1[:], AF.Exp, scale=0.125))
                        e2_i = se.activation(pb2[:], ps2[:], AF.Exp, scale=0.125)
                        self.ab_pex[j % 2] = self.ev.inc(e2_i)
                        d = j - (nch - 4)
                        if d >= 0:
                            self.ev.w(ve)
                            mk = self.c_masks[:, d * SW:(d + 1) * SW]
                            self.vf.inc(ve.tensor_tensor(pb1[:], pb1[:], mk, ALU.mult))
                            self.vf.inc(ve.tensor_tensor(pb2[:], pb2[:], mk, ALU.mult))
                            self.ab_pmv[j % 2] = self.vf.n

                    def back_b(j):
                        g, j4 = j // 4, j % 4
                        vb_ = vt[g % 2][:, j4 * 128:(j4 + 1) * 128]
                        pb1, pb2 = p1[j % 2], p2[j % 2]
                        d = j - (nch - 4)
                        if d >= 0:
                            self.vf.w(te, self.ab_pmv[j % 2])
                        else:
                            self.ev.w(te, self.ab_pex[j % 2])
                        self.mm.inc(te.matmul(p_z[0:1, :], self.c_onesc[:], pb1[:],
                                              start=(j == 0), stop=(j == nch - 1)))
                        self.mm.inc(te.matmul(p_z[32:33, :], self.c_onesc[:], pb2[:],
                                              start=(j == 0), stop=(j == nch - 1),
                                              tile_position=(0, 32)))
                        self.mm.inc(te.matmul(p_a1[:], vb_, pb1[:],
                                              start=(j == 0), stop=(j == nch - 1)))
                        inst = te.matmul(p_a2[:], vb_, pb2[:],
                                         start=(j == 0), stop=(j == nch - 1))
                        self.ab_pav[j % 2] = self.mm.inc(inst)
                        if j4 == 3 or j == nch - 1:
                            self.ab_kld[g % 2] = self.mm.n

                    for j in range(nch):
                        front_b(j)
                        if j > 0:
                            back_b(j - 1)
                    back_b(nch - 1)
                    p_b1, p_b2 = p_s1[0], p_s1[1]
                    self.mm.w(se)
                    self.st.w(se)
                    self.ev.inc(se.copy(ev1[:], p_a1[:]))
                    self.ev.inc(se.copy(ev2[:], p_a2[:]))
                    self.ev.inc(se.copy(R_Z1, p_z[0:1, :]))
                    self.ev.inc(se.copy(R_Z2, p_z[32:33, :]))
                    self.ev.w(ve)
                    self.st.w(ve)
                    self.vf.inc(ve.reciprocal(R_Z1, R_Z1))
                    self.vf.inc(ve.reciprocal(R_Z2, R_Z2))
                    self.vf.w(te)
                    self.ev.w(te)
                    self.mm.inc(te.matmul(p_b1[:], self.c_ones1[:], R_Z1,
                                          start=True, stop=True))
                    self.mm.inc(te.matmul(p_b2[:], self.c_ones1[:], R_Z2,
                                          start=True, stop=True))
                    self.mm.w(ve)
                    self.vf.inc(ve.tensor_tensor(t1[:], ev1[:], p_b1[:], ALU.mult))
                    self.vf.inc(ve.tensor_tensor(t2[:], ev2[:], p_b2[:], ALU.mult))
                    self.vf.w(ve)
                    self.vf.inc(ve.tensor_scalar(t2[:], t2[:], self.c_lam[:], None,
                                                 ALU.mult))
                    self.vf.w(ve)
                    A_i = ve.tensor_tensor(At[:], t1[:], t2[:], ALU.subtract)
                    self.vf.inc(A_i)
                    self.vf.w(se)
                    self.vf.w(te)
                    self.ev.inc(se.activation(asq[:], At[:], AF.Square))
                    self.ev.w(te)
                    self.mm.inc(te.matmul(p_z[0:1, :], self.c_onesc[:], At[:],
                                          start=True, stop=True))
                    self.mm.inc(te.matmul(p_z[32:33, :], self.c_onesc[:], asq[:],
                                          start=True, stop=True,
                                          tile_position=(0, 32)))
                    self.mm.w(ve)
                    self.vf.inc(ve.tensor_scalar(R_MU, p_z[0:1, :], 1.0 / 128.0,
                                                 None, ALU.mult))
                    self.vf.inc(ve.tensor_scalar(R_E2, p_z[32:33, :], 1.0 / 128.0,
                                                 None, ALU.mult))
                    self.vf.w(ge)
                    self.pf.inc(ge.tensor_tensor(R_MR, R_MU, R_MU, ALU.mult))
                    self.pf.w(ve)
                    self.vf.inc(ve.tensor_tensor(R_R, R_E2, R_MR, ALU.subtract))
                    self.vf.w(se)
                    self.ev.inc(se.activation(R_R, R_R, AF.Sqrt, bias=self.c_eps_ln[:]))
                    self.ev.w(ve)
                    self.vf.inc(ve.reciprocal(R_R, R_R))
                    self.vf.w(ge)
                    self.pf.inc(ge.tensor_tensor(R_MR, R_MU, R_R, ALU.mult))
                    self.vf.w(te)
                    self.pf.w(te)
                    self.mm.inc(te.matmul(p_b1[:], self.c_ones1[:], R_R,
                                          start=True, stop=True))
                    self.mm.inc(te.matmul(p_b2[:], self.c_ones1[:], R_MR,
                                          start=True, stop=True))
                    self.mm.w(ve)
                    self.vf.w(ve)
                    self.vf.inc(ve.tensor_tensor(o1[:], At[:], p_b1[:], ALU.mult))
                    self.vf.w(ve)
                    self.vf.inc(ve.tensor_tensor(o1[:], o1[:], p_b2[:], ALU.subtract))
                    self.vf.w(ve)
                    self.vf.inc(ve.tensor_scalar(o1[:], o1[:],
                                                 self.c_gam[:, h:h + 1],
                                                 self.c_bet[:, h:h + 1],
                                                 ALU.mult, ALU.add))
                    self.vf.w(ge)
                    self.st.inc(ge.dma_start(
                        out=self.acat[h * DH:(h + 1) * DH, s * SW:(s + 1) * SW],
                        in_=o1[:]), 16)

    def ph_oproj(self, wdram):
        NKC, NOC = HL * DH // 128, NKD
        self._init_proj_state()

        def wsrc(k):
            return wdram[128 * k:128 * (k + 1), :]

        def src(k, s):
            return self.acat[128 * k:128 * (k + 1), s * SW:(s + 1) * SW]

        def dst(o, s):
            half, ls = s // NSL, s % NSL
            return self.opart[half * D + o * 128: half * D + (o + 1) * 128,
                              ls * SW:(ls + 1) * SW]

        self.ph_proj_fm(wsrc, src, dst, NKC, NOC, NSG)

    def ph_rs(self):
        ge = self.ge
        self.st.w(ge)
        cc = self.ccs[self.ncc]
        self.ncc += 1
        grp = [[0, 1], [2, 3], [4, 5], [6, 7]]
        cc.incc(ge.collective_compute("ReduceScatter", ALU.add, replica_groups=grp,
                                      ins=[self.opart], outs=[self.rsout]))
        cc.w(ge)

    def ph_gu(self):
        nc, sy, te, ve, se, ge = self.nc, self.sy, self.te, self.ve, self.se, self.ge
        self.cc_gu.w(sy)
        self.cc_gu.w(ge)
        with ExitStack() as p:
            wtg = [self.sb(f"gu_wg{i}", [128, NKD * 128], FP16, es=p) for i in range(2)]
            wtu = [self.sb(f"gu_wu{i}", [128, NKD * 128], FP16, es=p) for i in range(2)]
            ysf = [self.sb(f"gu_yf{i}", [128, SW], es=p) for i in range(2)]
            ys1 = self.sb("gu_y1", [128, NKD * NSL * SW], FP16, es=p)
            ys2 = self.sb("gu_y2", [128, NKD * NSL * SW], FP16, es=p)
            sg = [self.sb(f"gu_sg{i}", [128, SW], es=p) for i in range(2)]
            hh = [self.sb(f"gu_h{i}", [128, SW], es=p) for i in range(NSL)]
            h16 = [self.sb(f"gu_h16{i}", [128, 2 * SW], FP16, es=p) for i in range(NSL)]
            hq = [self.sb(f"gu_hq{i}", [128, SW], es=p) for i in range(2)]
            p_g = [self.ps(f"gu_pg{i}", [128, SW], es=p) for i in range(2)]
            p_u = [self.ps(f"gu_pu{i}", [128, SW], es=p) for i in range(2)]
            p_ss = [self.ps(f"gu_ss{i}", [1, SW], es=p) for i in range(NSL)]
            ysplit_done = {}
            for s in range(NSL):
                for k in range(NKD):
                    par = k % 2
                    lsl = self.ld2[par]
                    sy.wait_ge(self.vf.h, ysplit_done.get(par, 0))
                    lsl.inc(sy.dma_start(
                        out=ysf[par][:],
                        in_=self.ybuf[128 * k:128 * (k + 1), s * SW:(s + 1) * SW]), 16)
                    lsl.w(ve)
                    c = slice((k * NSL + s) * SW, (k * NSL + s + 1) * SW)
                    self.vf.inc(ve.tensor_copy(ys1[:, c], ysf[par][:]))
                    self.vf.w(ve)
                    inst = ve.tensor_tensor(ys2[:, c], ysf[par][:], ys1[:, c],
                                            ALU.subtract)
                    ysplit_done[par] = self.vf.inc(inst)

            def yc1(k, s):
                return ys1[:, (k * NSL + s) * SW:(k * NSL + s + 1) * SW]

            def yc2(k, s):
                return ys2[:, (k * NSL + s) * SW:(k * NSL + s + 1) * SW]

            self.vf.w(te)
            hstate_ev = {}
            hstate_vf = {}
            for o in range(NH_CH):
                wg, wu = wtg[o % 2], wtu[o % 2]
                lsl = self.ld2[o % 2]
                if o >= 2:
                    self.mm.w(sy, self.gu_wdone[o - 2])
                for k in range(NKD):
                    gr, gc = (o * 128) // 2048, (o * 128) % 2048
                    ur, uc = ((HID + o * 128) // 2048, (HID + o * 128) % 2048)
                    lsl.inc(sy.dma_start(
                        out=wg[:, k * 128:(k + 1) * 128],
                        in_=self.wg_gu[gr * D + 128 * k: gr * D + 128 * (k + 1),
                                       gc:gc + 128]), 16)
                    lsl.inc(sy.dma_start(
                        out=wu[:, k * 128:(k + 1) * 128],
                        in_=self.wg_gu[ur * D + 128 * k: ur * D + 128 * (k + 1),
                                       uc:uc + 128]), 16)
                lsl.w(te)
                for s in range(NSL):
                    i = o * NSL + s
                    par = i % 2
                    self.ev.w(te, hstate_ev.get(par, 0))
                    self.vf.w(te, hstate_vf.get(par, 0))
                    for k in range(NKD):
                        wk = wg[:, k * 128:(k + 1) * 128]
                        te.matmul(p_g[par][:], wk, yc1(k, s), start=(k == 0), stop=False)
                        te.matmul(p_g[par][:], wk, yc2(k, s), start=False,
                                  stop=(k == NKD - 1))
                    for k in range(NKD):
                        wk = wu[:, k * 128:(k + 1) * 128]
                        te.matmul(p_u[par][:], wk, yc1(k, s), start=(k == 0), stop=False)
                        inst = te.matmul(p_u[par][:], wk, yc2(k, s), start=False,
                                         stop=(k == NKD - 1))
                    self.mm.inc(inst)
                    self.mm.w(se)
                    self.vf.w(se, hstate_vf.get(par, 0))
                    self.ev.inc(se.activation(sg[par][:], p_g[par][:], AF.Silu))
                    self.ev.w(ve)
                    hb = hh[s]
                    hsl = self.st4[s]
                    hsl.w(ve)
                    h_i = ve.tensor_tensor(hb[:], sg[par][:], p_u[par][:], ALU.mult)
                    self.vf.inc(h_i)
                    self.vf.w(ge)
                    self.pf.inc(ge.tensor_copy(h16[s][:, :SW], hb[:]))
                    self.pf.w(ve)
                    self.vf.inc(ve.tensor_tensor(h16[s][:, SW:], hb[:], h16[s][:, :SW],
                                                 ALU.subtract))
                    hstate_vf[par] = self.vf.n
                    self.vf.w(ge)
                    self.vf.w(se)
                    self.pf.w(se)
                    hsl.inc(ge.dma_start(
                        out=self.hbuf1[o * 128:(o + 1) * 128, s * SW:(s + 1) * SW],
                        in_=h16[s][:, :SW]), 16)
                    hsl.inc(ge.dma_start(
                        out=self.hbuf2[o * 128:(o + 1) * 128, s * SW:(s + 1) * SW],
                        in_=h16[s][:, SW:]), 16)
                    self.mm.w(se)
                    hstate_ev[par] = self.ev.inc(se.activation(hq[par][:], hb[:], AF.Square))
                    self.ev.w(te)
                    inst = te.matmul(p_ss[s][:], self.c_onesc[:], hq[par][:],
                                     start=(o == 0), stop=(o == NH_CH - 1))
                    self.mm.inc(inst)
                self.gu_wdone[o] = self.mm.n
            self.mm.w(se)
            with ExitStack() as p2:
                r0 = self.sb("gu_r0", [1, NSL * SW], es=p2)
                for s in range(NSL):
                    self.ev.inc(se.copy(r0[:, s * SW:(s + 1) * SW], p_ss[s][:]))
                self.ev.w(ve)
                self.vf.inc(ve.tensor_scalar(r0[:], r0[:], 1.0 / HID, None, ALU.mult))
                self.vf.w(se)
                self.ev.inc(se.activation(r0[:], r0[:], AF.Sqrt, bias=self.c_eps_rms[:]))
                self.ev.w(ve)
                self.vf.inc(ve.reciprocal(self.c_inv[:], r0[:]))
                self.vf.w(se)
                self.vf.w(te)

    def ph_proj_ffn(self):
        nc, sy, te, ve, se, ge = self.nc, self.sy, self.te, self.ve, self.se, self.ge
        self.cc_pr.w(sy)
        with ExitStack() as p:
            wt4 = [self.sb(f"fp_w{i}", [128, NH_CH * 128], FP16, es=p) for i in range(4)]
            hs = [self.sb(f"fp_h{i}", [128, 2 * SW], FP16, es=p) for i in range(2)]
            og = [self.sb(f"fp_o{i}", [128, SW], es=p) for i in range(2)]
            ot = [self.sb(f"fp_ot{i}", [128, SW], es=p) for i in range(2)]
            pp = [self.ps(f"fp_p{i}", [128, SW], es=p) for i in range(4)]
            p_iv = [self.ps(f"fp_iv{i}", [128, SW], es=p) for i in range(NSL)]
            self.vf.w(te)
            for s in range(NSL):
                self.mm.inc(te.matmul(p_iv[s][:], self.c_ones1[:],
                                      self.c_inv[:, s * SW:(s + 1) * SW],
                                      start=True, stop=True))
            wdone = 0
            for ob in range(NKD // 4):
                self.mm.w(sy, wdone)
                for t in range(4):
                    o = 4 * ob + t
                    wsl = self.ld2[t % 2]
                    gr, gc = (o * 128) // 256, (o * 128) % 256
                    for kh in range(NH_CH):
                        wsl.inc(sy.dma_start(
                            out=wt4[t][:, kh * 128:(kh + 1) * 128],
                            in_=self.wg_pr[gr * HID + 128 * kh:
                                           gr * HID + 128 * (kh + 1),
                                           gc:gc + 128]), 16)
                for s in range(NSL):
                    self.ev.w(te)
                    if s == 0:
                        self.ld2[0].w(te)
                        self.ld2[1].w(te)
                    for kh in range(NH_CH):
                        hb = hs[kh % 2]
                        hsl = self.ak[kh % 2]
                        self.mm.w(sy, self.fp_hld.get(kh % 2, 0))
                        hsl.inc(sy.dma_start(
                            out=hb[:, :SW],
                            in_=self.hbuf1[kh * 128:(kh + 1) * 128,
                                           s * SW:(s + 1) * SW]), 16)
                        hsl.inc(sy.dma_start(
                            out=hb[:, SW:],
                            in_=self.hbuf2[kh * 128:(kh + 1) * 128,
                                           s * SW:(s + 1) * SW]), 16)
                        hsl.w(te)
                        for t in range(4):
                            wk = wt4[t][:, kh * 128:(kh + 1) * 128]
                            te.matmul(pp[t][:], wk, hb[:, :SW],
                                      start=(kh == 0), stop=False)
                            inst = te.matmul(pp[t][:], wk, hb[:, SW:],
                                             start=False, stop=(kh == NH_CH - 1))
                        self.fp_hld[kh % 2] = self.mm.inc(inst)
                    self.mm.w(se)
                    for t in range(4):
                        o = 4 * ob + t
                        bank = t % 2
                        self.vf.w(se)
                        self.ev.inc(se.copy(og[bank][:], pp[t][:]))
                        self.ev.w(ve)
                        ssl = self.st2[bank]
                        ssl.w(ve)
                        o_i = ve.tensor_tensor(ot[bank][:], og[bank][:],
                                               p_iv[s][:], ALU.mult)
                        self.vf.inc(o_i)
                        self.vf.w(ge)
                        ssl.inc(ge.dma_start(
                            out=self.brout[o * 128:(o + 1) * 128,
                                           s * SW:(s + 1) * SW],
                            in_=ot[bank][:]), 16)
                wdone = self.mm.n

    def ph_final(self):
        nc, sy, ve, ge = self.nc, self.sy, self.ve, self.ge
        with ExitStack() as p:
            xa = self.sb("fin_x", [128, SW], es=p)
            xb = self.sb("fin_b", [128, SW], es=p)
            for s in range(NSL):
                for k in range(NKD):
                    cols = slice(s * SW, (s + 1) * SW)
                    self.ld.inc(sy.dma_start(out=xa[:], in_=self.xres[128 * k:128 * (k + 1), cols]), 16)
                    self.ld.inc(sy.dma_start(out=xb[:], in_=self.brout[128 * k:128 * (k + 1), cols]), 16)
                    self.ld.w(ve)
                    inst = ve.tensor_tensor(xa[:], xa[:], xb[:], ALU.add)
                    self.vf.inc(inst)
                    self.vf.w(ge)
                    self.st.inc(ge.dma_start(out=self.OUT[128 * k:128 * (k + 1), cols],
                                             in_=xa[:]), 16)
                    self.st.w(sy)
        self.st.w(ge)

    def ph_dump(self):
        if not self.dump:
            return
        ge = self.ge
        self.barrier()
        self.st.w(ge)
        for c in self.ccs:
            c.w(ge)
        for name in self.dump:
            src = getattr(self, name)
            shp = list(src.shape)
            o = self.nc.dram_tensor("dbg_" + name, shp, src.dtype,
                                    kind="ExternalOutput").ap()
            self.st.inc(ge.dma_start(out=o, in_=src), 16)
        self.st.w(ge)

    def build(self):
        ph = []

        def run(name, fn, *a):
            if self.stopped:
                return
            self.drain_barrier()
            fn(*a)
            if self.upto is not None and name == self.upto:
                self.stopped = True

        self.aa_vld = {}
        self.aa_kld = {}
        self.aa_pav = {}
        self.ab_pav = {}
        self.aa_pex = {}
        self.aa_smv = {}
        self.ab_kld = {}
        self.ab_pex = {}
        self.ab_pmv = {}
        self.gu_wdone = {}
        self.fp_hld = {}

        run("const", self.ph_const)
        run("quant", self.ph_quant)
        run("ln1", self.ph_ln, self.X, None, None, self.ybuf)
        run("gather1", self.ph_gather_y)
        run("qkvA", self._qkv, self.WqA, self.WkA, self.WvA)
        run("attnA", self.ph_attn_a)
        run("oprojA", self.ph_oproj, self.WoA)
        run("rsA", self.ph_rs)
        run("ln2", self.ph_ln, self.X, self.rsout, self.xres, self.ybuf)
        run("gu1", self.ph_gu)
        run("fproj1", self.ph_proj_ffn)
        run("ln3", self.ph_ln, self.xres, self.brout, self.xres, self.ybuf)
        run("gather3", self.ph_gather_y)
        run("qkvB", self._qkv, self.WqB, self.WkB, self.WvB)
        run("attnB", self.ph_attn_b)
        run("oprojB", self.ph_oproj, self.WoB)
        run("rsB", self.ph_rs)
        run("ln4", self.ph_ln, self.xres, self.rsout, self.xres, self.ybuf)
        run("gu2", self.ph_gu)
        run("fproj2", self.ph_proj_ffn)
        run("final", self.ph_final)
        self.ph_dump()
        return self.nc

    def _qkv(self, Wq, Wk, Wv):
        for wdram, dstbuf in ((Wq, self.qbuf), (Wk, self.kbuf)):
            NKC, NOC = NKD, HL * DH // 128
            self._init_proj_state()

            def wsrc(k, wd=wdram):
                return wd[128 * k:128 * (k + 1), :]

            def src(k, s):
                half = s // NSL
                ls = s % NSL
                return self.yall[half * D + 128 * k: half * D + 128 * (k + 1),
                                 ls * SW:(ls + 1) * SW]

            def dst(o, s, db=dstbuf):
                return db[o * 128:(o + 1) * 128, s * SW:(s + 1) * SW]

            self.ph_proj_fm(wsrc, src, dst, NKC, NOC, NSG)
        self._init_proj_state()
        self.ph_proj_tm(Wv, self.yall, self.vbuf)


def _host_consts():
    ones1 = np.ones((1, 128), np.float32)
    oh2 = np.full((128, 2), 0.5, np.float32)
    onesc = np.ones((128, 1), np.float32)
    ones1_64 = np.ones((1, 64), np.float32)
    gsel = np.zeros((128, 33), np.float32)
    gsel[0:64, 0] = 1.0
    gsel[64:128, 32] = 1.0
    masks = np.zeros((128, 4 * SW), np.float32)
    kk = np.arange(128)[:, None]
    qq = np.arange(SW)[None, :]
    for d in range(4):
        masks[:, d * SW:(d + 1) * SW] = (kk <= (qq - 128 * d)).astype(np.float32)
    return ones1, onesc, oh2, ones1_64, gsel, masks


def _make_in_maps(inputs):
    x = np.asarray(inputs["L0_attn_a_x"], np.float32)
    WqA = np.asarray(inputs["L0_attn_a_Wq"], np.float32)
    WkA = np.asarray(inputs["L0_attn_a_Wk"], np.float32)
    WvA = np.asarray(inputs["L0_attn_a_Wv"], np.float32)
    WoA = np.asarray(inputs["L0_attn_a_Wo"], np.float32)
    Wqkv = np.asarray(inputs["L0_attn_b_W_qkv"], np.float32)
    WoB = np.asarray(inputs["L0_attn_b_W_o"], np.float32)
    WguT = np.ascontiguousarray(np.asarray(inputs["L0_ffn_w_gate_up"], np.float32).T)
    WpT = np.ascontiguousarray(np.asarray(inputs["L0_ffn_w_proj"], np.float32).T)
    lq1 = np.asarray(inputs["L0_attn_b_lambda_q1"], np.float32).reshape(-1, 1)
    lk1 = np.asarray(inputs["L0_attn_b_lambda_k1"], np.float32).reshape(-1, 1)
    lq2 = np.asarray(inputs["L0_attn_b_lambda_q2"], np.float32).reshape(-1, 1)
    lk2 = np.asarray(inputs["L0_attn_b_lambda_k2"], np.float32).reshape(-1, 1)
    gam = np.asarray(inputs["L0_attn_b_gn_gamma"], np.float32)
    bet = np.asarray(inputs["L0_attn_b_gn_beta"], np.float32)
    WqkvT = np.ascontiguousarray(Wqkv.T)
    WoBT = np.ascontiguousarray(WoB.T)
    ones1, onesc, oh2, ones1_64, gsel, masks = _host_consts()

    in_maps = []
    for c in range(8):
        b, half = c // 2, c % 2
        hs = slice(half * HL * DH, (half + 1) * HL * DH)
        m = dict(
            X=np.ascontiguousarray(x[b, half * TL:(half + 1) * TL, :].T),
            WqA=np.ascontiguousarray(WqA[:, hs]),
            WkA=np.ascontiguousarray(WkA[:, hs]),
            WvA=np.ascontiguousarray(WvA[:, hs]),
            WoA=np.ascontiguousarray(WoA[hs, :]),
            WqB=np.ascontiguousarray(WqkvT[:, half * HL * DH:(half + 1) * HL * DH]),
            WkB=np.ascontiguousarray(WqkvT[:, D + half * HL * DH: D + (half + 1) * HL * DH]),
            WvB=np.ascontiguousarray(WqkvT[:, 2 * D + half * HL * DH: 2 * D + (half + 1) * HL * DH]),
            WoB=np.ascontiguousarray(WoBT[hs, :]),
            WguT=np.ascontiguousarray(WguT[:, c * 2048:(c + 1) * 2048]),
            WpT=np.ascontiguousarray(WpT[:, c * 256:(c + 1) * 256]),
            LQ1=lq1, LK1=lk1, LQ2=lq2, LK2=lk2,
            GAM=np.ascontiguousarray(gam[half * HL:(half + 1) * HL, :].T),
            BET=np.ascontiguousarray(bet[half * HL:(half + 1) * HL, :].T),
            ONES1=ones1, ONESC=onesc, OH2=oh2, ONES1_64=ones1_64, GSEL=gsel, MASKS=masks,
        )
        in_maps.append(m)
    return in_maps


def build_nc(upto=None, dump=()):
    es = ExitStack()
    with es:
        kb = KB(es, upto=upto, dump=dump)
        nc = kb.build()
    return nc


def kernel(**inputs):
    in_maps = _make_in_maps(inputs)
    nc = build_nc()
    res = run_bass_kernel_spmd(nc, in_maps, core_ids=list(range(8)))
    out = np.empty((B, T, D), np.float32)
    for c in range(8):
        b, half = c // 2, c % 2
        out[b, half * TL:(half + 1) * TL, :] = np.asarray(res.results[c]["OUT"]).T
    return out
